# version 17 (final)
# baseline (speedup 1.0000x reference)
import numpy as np
import concourse.bass as bass
import concourse.mybir as mybir
from concourse.bass_utils import run_bass_kernel_spmd

F32 = mybir.dt.float32
BF16 = mybir.dt.bfloat16
U8 = mybir.dt.uint8
AF = mybir.ActivationFunctionType
ALU = mybir.AluOpType
AX = mybir.AxisListType.X

NCORES = 8
NSEQ = 4
T = 2048
C = 1024
NT = T // 128
H = 8
DH = 64
E = 16
EH = 256
SHID = 2048
KC = C // 128
NKC_H = (SHID + E * EH) // 128
RSQRT2 = float(2.0 ** -0.5)
USE_BOUND = True
LN_EPS = 1e-5
KB = 1024


class Sched:
    ENGINES = ("T", "V", "S", "G", "SY")

    def __init__(self):
        self.ops = {e: [] for e in self.ENGINES}
        self.cnt = {"T": 0, "V": 0, "S": 0, "G": 0, "SA": 0, "SG": 0,
                    "SC": 0, "SA2": 0, "SG2": 0, "SC2": 0, "SD": 0}

    def emit(self, eng, fn, waits=(), inc=None):
        waits = [(s, v) for (s, v) in waits if v and v > 0]
        for s, v in waits:
            assert v <= self.cnt[s], (s, v, self.cnt[s])
        after = None
        if inc in ("SA", "SG", "SC", "SA2", "SG2", "SC2", "SD"):
            if self.cnt[inc] > 0:
                waits = [(inc, self.cnt[inc])] + waits
        elif inc in ("V", "S", "G"):
            if self.cnt[inc] > 0:
                waits = [(inc, self.cnt[inc])] + waits
        if inc is not None:
            self.cnt[inc] += 16 if inc not in ("T", "V", "S", "G") else 1
            after = self.cnt[inc]
        self.ops[eng].append((waits, fn, inc, after))
        return after

    def now(self, sem):
        return self.cnt[sem]

    def bar(self):
        return dict(self.cnt)


def _emit_engine(sched, eng_name, eng, sems):
    last_wait = {}
    for waits, fn, inc, after in sched.ops[eng_name]:
        for s, v in waits:
            if last_wait.get(s, -1) >= v:
                continue
            eng.wait_ge(sems[s], v)
            last_wait[s] = v
        inst = fn(eng)
        if inc is not None:
            assert inst is not None
            inst.then_inc(sems[inc],
                          16 if inc not in ("T", "V", "S", "G") else 1)


class Ring:
    def __init__(self, n):
        self.free = [()] * n
        self.n = n

    def waits(self, i):
        return list(self.free[i % self.n])

    def set_free(self, i, *deps):
        self.free[i % self.n] = deps

    def reset(self, *deps):
        self.free = [deps] * self.n


def build_nc(n_seq=NSEQ):
    nc = bass.Bass("TRN2", target_bir_lowering=False)

    x_in = nc.dram_tensor("x", [n_seq * T, C], F32, kind="ExternalInput").ap()
    wqk = nc.dram_tensor("wqk", [C, 1024], F32, kind="ExternalInput").ap()
    bqk = nc.dram_tensor("bqk", [1, 1024], F32, kind="ExternalInput").ap()
    wv = nc.dram_tensor("wv", [C, 512], F32, kind="ExternalInput").ap()
    bv = nc.dram_tensor("bv", [1, 512], F32, kind="ExternalInput").ap()
    wout = nc.dram_tensor("wout", [512, C], F32, kind="ExternalInput").ap()
    bout = nc.dram_tensor("bout", [1, C], F32, kind="ExternalInput").ap()
    gam = nc.dram_tensor("gam", [1, C], F32, kind="ExternalInput").ap()
    bet = nc.dram_tensor("bet", [1, C], F32, kind="ExternalInput").ap()
    rw = nc.dram_tensor("rw", [16, C], F32, kind="ExternalInput").ap()
    cb = nc.dram_tensor("cb", [1, 16], F32, kind="ExternalInput").ap()
    gw = nc.dram_tensor("gw", [E, C, EH], F32, kind="ExternalInput").ap()
    uw = nc.dram_tensor("uw", [E, C, EH], F32, kind="ExternalInput").ap()
    dw = nc.dram_tensor("dw", [E, EH, C], F32, kind="ExternalInput").ap()
    sgw = nc.dram_tensor("sgw", [C, SHID], F32, kind="ExternalInput").ap()
    suw = nc.dram_tensor("suw", [C, SHID], F32, kind="ExternalInput").ap()
    sdw = nc.dram_tensor("sdw", [SHID, C], F32, kind="ExternalInput").ap()
    y_out = nc.dram_tensor("y", [n_seq * T, C], F32, kind="ExternalOutput").ap()
    h_cache = nc.dram_tensor("h_cache", [NKC_H * 128, T], BF16, kind="Internal").ap()
    x1_cache = nc.dram_tensor("x1_cache", [T, C], F32, kind="Internal").ap()

    ARENA_KB = 204
    arena_h = nc.alloc_sbuf_tensor("arena", [128, ARENA_KB * KB], U8)

    def reg(off_kb, size_kb, dtype, shape):
        a = arena_h[:, off_kb * KB:(off_kb + size_kb) * KB].bitcast(dtype)
        n = a.shape[1]
        want = 1
        for sdim in shape[1:]:
            want *= sdim
        assert want == n, (shape, n, off_kb, size_kb)
        if len(shape) == 2:
            return a
        if len(shape) == 3:
            return a.rearrange("p (a b) -> p a b", b=shape[2])
        return a.rearrange("p (a b c) -> p a b c", b=shape[2], c=shape[3])

    consts = reg(0, 1, F32, [128, 256])
    identB = reg(1, 1, BF16, [128, 512])
    identF = reg(2, 1, F32, [128, 256])
    gamB = reg(3, 4, F32, [128, C])
    betB = reg(7, 4, F32, [128, C])
    boutB = reg(11, 4, F32, [128, C])
    bvB = reg(15, 2, F32, [128, 512])
    cbB = reg(17, 1, F32, [128, 256])
    bqk_col = reg(18, 1, F32, [128, 256])
    rwT = reg(19, 1, BF16, [128, 8, 64])
    stats = reg(20, 1, F32, [128, 256])
    (MU1, RSTD1, NEGMUR1, MUZ, RSTDZ, NEGMURZ, MU2, RSTD2, NEGMUR2,
     RSTD1S, NEGMUR1S) = (0, 16, 32, 48, 64, 80, 96, 112, 128, 144, 160)
    rs_t = reg(21, 1, F32, [128, 256])
    combF = reg(22, 4, BF16, [128, T])
    r_s = reg(26, 1, F32, [128, 256])
    r_sb = reg(27, 1, F32, [128, 256])
    r_scr = reg(28, 2, F32, [128, 512])
    r_gs = reg(30, 1, F32, [128, 256])
    r_msk = reg(31, 1, F32, [128, 256])
    r_c16 = reg(32, 1, F32, [128, 256])
    r_w = reg(33, 1, F32, [128, 256])
    r_ws = reg(34, 1, F32, [128, 256])
    comb_tm = reg(35, 1, BF16, [128, 512])
    ones_bf = reg(36, 1, BF16, [128, 512])
    ones_f = reg(37, 1, F32, [128, 256])
    ones_bf2 = reg(36, 1, BF16, [128, 512])

    lnF = reg(38, 32, BF16, [128, KC, T])
    rw_sb = reg(38, 4, F32, [128, C])
    h_ring = reg(38, 24, BF16, [128, 2, NKC_H, 128])
    x1e_ring = reg(62, 8, F32, [128, 2, C])

    x_ring = reg(70, 8, F32, [128, 2, C])
    cB_sb = reg(70, 4, BF16, [128, T])
    cb2_sb = reg(74, 4, BF16, [128, T])
    lnx_ring = reg(78, 4, BF16, [128, 2, C])

    wqk_sb = reg(82, 16, BF16, [128, KC, 1024])
    wv_sb = reg(98, 8, BF16, [128, KC, 512])
    wout_sb = reg(106, 8, BF16, [128, 4, 1024])
    qF = reg(114, 16, BF16, [128, 4, T])
    lgF = reg(114, 8, F32, [128, T])
    kF = reg(130, 16, BF16, [128, 4, T])
    v_tm = reg(146, 16, BF16, [128, NT, 512])
    oF = reg(162, 16, BF16, [128, 4, T])
    wd_sb = reg(82, 96, BF16, [128, NKC_H, 1024])

    P_A = reg(178, 4, BF16, [128, T])
    P_B = reg(182, 4, BF16, [128, T])
    PtA = reg(186, 4, BF16, [128, T])
    PtB = reg(190, 4, BF16, [128, T])
    z1_ring = reg(178, 8, F32, [128, 2, C])
    scr1 = reg(186, 8, F32, [128, 2, C])
    x1w = reg(194, 8, F32, [128, 2, C])
    sq_dump = reg(202, 2, BF16, [128, 1024])
    swg_ring = reg(178, 8, BF16, [128, 4, KC, 128])
    swu_ring = reg(186, 8, BF16, [128, 4, KC, 128])
    wg_buf = reg(178, 8, BF16, [128, 2, KC, EH])
    wu_buf = reg(186, 8, BF16, [128, 2, KC, EH])
    h_stage = reg(194, 8, BF16, [128, 2, T])
    silu_scr = reg(202, 2, BF16, [128, 2, 512])
    x2_ring = reg(178, 8, F32, [128, 2, C])

    ps = nc.alloc_psum_tensor("ps", [128, 4096], F32)
    B0, B2 = 0, 1024
    TP0, TP1 = 2048, 3072

    CSCL, CEPS, CONE, CZERO, CR2, CINV, CTINY = range(7)

    sched = Sched()
    EMIT = sched.emit

    def dma_sy(dst, src):
        def f(sy):
            return sy.dma_start(out=dst, in_=src)
        return f

    def dma_g(dst, src):
        def f(g):
            return g.dma_start(out=dst, in_=src)
        return f

    def f(v):
        v.memset(consts[:, CSCL:CSCL + 1], 0.125)
        v.memset(consts[:, CEPS:CEPS + 1], LN_EPS)
        v.memset(consts[:, CONE:CONE + 1], 1.0)
        v.memset(consts[:, CZERO:CZERO + 1], 0.0)
        v.memset(consts[:, CR2:CR2 + 1], RSQRT2)
        v.memset(consts[:, CINV:CINV + 1], 1.0 / C)
        v.memset(consts[:, CTINY:CTINY + 1], 1e-20)
        v.memset(ones_bf[0:1, 0:128], 1.0)
        v.memset(ones_bf[:, 256:257], 1.0)
        return v.memset(ones_f[0:1, 0:128], 1.0)
    v_init = EMIT("V", f, inc="V")

    def f(g):
        g.iota(identB[:, 128:256], pattern=[[1, 128]], base=0,
               channel_multiplier=-1, allow_small_or_imprecise_dtypes=True)
        return g.iota(identF[:, 128:256], pattern=[[1, 128]], base=0,
                      channel_multiplier=-1,
                      allow_small_or_imprecise_dtypes=True)
    g_iota = EMIT("G", f, inc="G")

    def f(v):
        v.tensor_single_scalar(identB[:, 0:128], identB[:, 128:256], 0.0,
                               ALU.is_equal)
        return v.tensor_single_scalar(identF[:, 0:128], identF[:, 128:256], 0.0,
                                      ALU.is_equal)
    v_ident = EMIT("V", f, waits=[("G", g_iota)], inc="V")

    EMIT("G", dma_g(wqk_sb[:], wqk.rearrange("(kc p) f -> p kc f", p=128)),
         inc="SG")
    EMIT("G", dma_g(wv_sb[:], wv.rearrange("(kc p) f -> p kc f", p=128)),
         inc="SG")
    sg_attnw = EMIT("G", dma_g(wout_sb[:], wout.rearrange("(kc p) f -> p kc f",
                                                          p=128)), inc="SG")

    EMIT("SY", dma_sy(gamB[0:1, :], gam[:, :]), inc="SA")
    EMIT("SY", dma_sy(betB[0:1, :], bet[:, :]), inc="SA")
    EMIT("SY", dma_sy(boutB[0:1, :], bout[:, :]), inc="SA")
    EMIT("SY", dma_sy(bvB[0:1, 0:512], bv[:, :]), inc="SA")
    EMIT("SY", dma_sy(cbB[0:1, 0:16], cb[:, :]), inc="SA")
    EMIT("SY", dma_sy(rw_sb[0:16, :], rw[:, :]), inc="SA")
    bqk_row = lgF[0:1, 0:1024]
    sa_rows = EMIT("SY", dma_sy(bqk_row, bqk[:, :]), inc="SA")

    prev_v = [v_init]

    def bcast(dst, src_row, n, extra=()):
        t_marks = []
        for j in range(0, n, 512):
            w = min(512, n - j)
            def f(t, j=j, w=w, src_row=src_row):
                return t.matmul(ps[:, j:j + w], ones_f[0:1, 0:128],
                                src_row[:, j:j + w], start=True, stop=True)
            wl = [("SA", sa_rows), ("V", prev_v[0])] + list(extra)
            t_marks.append(EMIT("T", f, waits=wl, inc="T"))
        for i, j in enumerate(range(0, n, 512)):
            w = min(512, n - j)
            def f(v, j=j, w=w, dst=dst):
                return v.tensor_copy(dst[:, j:j + w], ps[:, j:j + w])
            prev_v[0] = EMIT("V", f, waits=[("T", t_marks[i])], inc="V")

    bcast(gamB, gamB[0:1, :], C)
    bcast(betB, betB[0:1, :], C)
    bcast(boutB, boutB[0:1, :], C)
    bcast(bvB, bvB[0:1, 0:512], 512)
    bcast(cbB[:, 0:16], cbB[0:1, 0:16], 16)

    def f(v):
        v.tensor_scalar_mul(gamB[:, :], gamB[:, :], RSQRT2)
        return v.tensor_scalar_mul(betB[:, :], betB[:, :], RSQRT2)
    prev_v[0] = EMIT("V", f, waits=[("V", prev_v[0])], inc="V")

    def f(t):
        li = None
        for m in range(8):
            li = t.matmul(ps[:, 2048 + m:2048 + m + 1],
                          bqk_row[:, 128 * m:128 * (m + 1)],
                          ones_f[0:1, 0:1], start=(m == 0), stop=(m == 7))
        return li
    t_bq = EMIT("T", f, waits=[("SA", sa_rows), ("V", prev_v[0])], inc="T")

    def f(v):
        return v.tensor_copy(bqk_col[:, 0:8], ps[:, 2048:2056])
    EMIT("V", f, waits=[("T", t_bq)], inc="V")

    def f(t):
        li = None
        for c in range(KC):
            li = t.matmul(ps[:, 2560 + 16 * c:2560 + 16 * (c + 1)],
                          rw_sb[0:16, 128 * c:128 * (c + 1)],
                          identF[0:16, 0:16], start=(c == 0), stop=(c == KC - 1))
        return li
    t_rw = EMIT("T", f, waits=[("SA", sa_rows), ("V", v_ident)], inc="T")

    def f(v):
        return v.tensor_copy(rwT[:, :, 0:16],
                             ps[:, 2560:2688].rearrange("p (a b) -> p a b", b=16))
    EMIT("V", f, waits=[("T", t_rw)], inc="V")

    xr, lr, tpr, p0r = Ring(2), Ring(2), Ring(2), Ring(2)
    z1r, hr, x1er, x2r = Ring(2), Ring(2), Ring(2), Ring(2)
    swr, wgr = Ring(4), Ring(2)
    x1wr, scr1r = Ring(2), Ring(2)

    x_r3 = x_in.rearrange("(n p) c -> p n c", p=128)
    y_r3 = y_out.rearrange("(n p) c -> p n c", p=128)
    h_r3 = h_cache.rearrange("(kc p) tok -> p kc tok", p=128)
    x1_r3 = x1_cache.rearrange("(t p) c -> p t c", p=128)

    def ln_stats(MU, RSTD, NEGMUR, t, src, pre_waits, wfn=None):
        def f(v, t=t):
            v.reduce_sum(stats[:, MU + t:MU + t + 1], src, axis=AX)
            return v.scalar_tensor_tensor(scr1[:, 0, :], src, 0.0, src, ALU.bypass,
                                          ALU.mult,
                                          accum_out=stats[:, RSTD + t:RSTD + t + 1])
        wl = (wfn("V", pre_waits) if wfn else list(pre_waits))
        v_a = EMIT("V", f, waits=wl, inc="V")

        def f(v, t=t):
            return v.tensor_scalar_mul(stats[:, MU + t:MU + t + 1],
                                       stats[:, MU + t:MU + t + 1], 1.0 / C)
        v_b = EMIT("V", f, waits=[("V", v_a)], inc="V")

        def f(v, t=t):
            return v.scalar_tensor_tensor(stats[:, NEGMUR + t:NEGMUR + t + 1],
                                          stats[:, MU + t:MU + t + 1], 1.0,
                                          stats[:, MU + t:MU + t + 1],
                                          ALU.mult, ALU.mult)
        v_c = EMIT("V", f, waits=[("V", v_b)], inc="V")

        def f(v, t=t):
            return v.scalar_tensor_tensor(
                stats[:, RSTD + t:RSTD + t + 1],
                stats[:, RSTD + t:RSTD + t + 1], 1.0 / C,
                stats[:, NEGMUR + t:NEGMUR + t + 1], ALU.mult, ALU.subtract)
        v_var = EMIT("V", f, waits=[("V", v_c)], inc="V")

        def f(sc, t=t):
            return sc.activation(stats[:, RSTD + t:RSTD + t + 1],
                                 stats[:, RSTD + t:RSTD + t + 1], AF.Sqrt,
                                 bias=consts[:, CEPS:CEPS + 1])
        s_sq = EMIT("S", f, waits=[("V", v_var)], inc="S")

        def f(v, t=t):
            return v.reciprocal(stats[:, RSTD + t:RSTD + t + 1],
                                stats[:, RSTD + t:RSTD + t + 1])
        v_rc = EMIT("V", f, waits=[("S", s_sq)], inc="V")

        def f(v, t=t):
            return v.scalar_tensor_tensor(
                stats[:, NEGMUR + t:NEGMUR + t + 1],
                stats[:, MU + t:MU + t + 1], -1.0,
                stats[:, RSTD + t:RSTD + t + 1], ALU.mult, ALU.mult)
        return EMIT("V", f, waits=[("V", v_rc)], inc="V")

    for s in range(n_seq):
        BAR = sched.bar()
        barw = [(k, BAR[k]) for k in ("T", "V", "S", "G", "SA", "SG", "SC")]
        first = {e: True for e in ("T", "V", "S", "G", "SY")}

        def W(eng, waits):
            if first[eng]:
                first[eng] = False
                return barw + list(waits)
            return list(waits)

        v_evacA = None
        for t in range(NT):
            xl = EMIT("SY", dma_sy(x_ring[:, t % 2, :], x_r3[:, s * NT + t, :]),
                      waits=W("SY", xr.waits(t)),
                      inc="SA" if t % 2 == 0 else "SA2")

            if first["S"]:
                EMIT("S", lambda sc: sc.nop(), waits=W("S", []))
            v_st = ln_stats(MU1, RSTD1, NEGMUR1, t, x_ring[:, t % 2, :],
                            [("SA" if t % 2 == 0 else "SA2", xl)], wfn=W)

            def f(sc, t=t):
                return sc.activation(lnx_ring[:, t % 2, :], x_ring[:, t % 2, :],
                                     AF.Identity,
                                     bias=stats[:, NEGMUR1 + t:NEGMUR1 + t + 1],
                                     scale=stats[:, RSTD1 + t:RSTD1 + t + 1])
            s_ln1 = EMIT("S", f, waits=lr.waits(t) + [("V", v_st)], inc="S")
            xr.set_free(t, ("S", s_ln1))

            base = TP0 if t % 2 == 0 else TP1
            def f(tn, t=t, base=base):
                li = None
                for c in range(KC):
                    li = tn.matmul(ps[:, base + 128 * c:base + 128 * (c + 1)],
                                   lnx_ring[:, t % 2, 128 * c:128 * (c + 1)],
                                   identB[:, 0:128],
                                   start=(c % 4 == 0), stop=(c % 4 == 3))
                return li
            t_tr = EMIT("T", f,
                        waits=W("T", tpr.waits(t) + [("S", s_ln1),
                                                     ("V", v_ident)]),
                        inc="T")
            lr.set_free(t, ("T", t_tr))

            def f(v, t=t, base=base):
                v.tensor_copy(lnF[:, 0:4, 128 * t:128 * (t + 1)],
                              ps[:, base:base + 512].rearrange(
                                  "p (a b) -> p a b", b=128))
                return v.tensor_copy(lnF[:, 4:8, 128 * t:128 * (t + 1)],
                                     ps[:, base + 512:base + 1024].rearrange(
                                         "p (a b) -> p a b", b=128))
            v_evacA = EMIT("V", f, waits=[("T", t_tr)], inc="V")
            tpr.set_free(t, ("V", v_evacA))
        v_lnF = v_evacA

        for m in range(8):
            for tb in range(4):
                i = m * 4 + tb
                base = B0 if i % 2 == 0 else B2
                wl = p0r.waits(i)
                def f(tn, m=m, tb=tb, base=base):
                    li = None
                    for kc in range(KC):
                        li = tn.matmul(ps[:, base:base + 512],
                                       wqk_sb[:, kc, 128 * m:128 * (m + 1)],
                                       lnF[:, kc, 512 * tb:512 * (tb + 1)],
                                       start=(kc == 0), stop=(kc == KC - 1))
                    return li
                t_mm = EMIT("T", f, waits=wl + [("V", v_lnF), ("SG", sg_attnw)],
                            inc="T")
                dst = qF if m < 4 else kF
                def f(v, m=m, tb=tb, base=base, dst=dst):
                    return v.tensor_scalar(dst[:, m % 4, 512 * tb:512 * (tb + 1)],
                                           ps[:, base:base + 512],
                                           bqk_col[:, m:m + 1], None, ALU.add)
                ev = EMIT("V", f, waits=[("T", t_mm)], inc="V")
                p0r.set_free(i, ("V", ev))
        for t in range(NT):
            base = B0 if t % 2 == 0 else B2
            wl = p0r.waits(t)
            def f(tn, t=t, base=base):
                li = None
                for kc in range(KC):
                    li = tn.matmul(ps[:, base:base + 512],
                                   lnF[:, kc, 128 * t:128 * (t + 1)],
                                   wv_sb[:, kc, :],
                                   start=(kc == 0), stop=(kc == KC - 1))
                return li
            t_mm = EMIT("T", f, waits=wl, inc="T")
            def f(v, t=t, base=base):
                return v.scalar_tensor_tensor(v_tm[:, t, :], ps[:, base:base + 512],
                                              0.0, bvB[:, 0:512], ALU.bypass,
                                              ALU.add)
            ev = EMIT("V", f, waits=[("T", t_mm)], inc="V")
            p0r.set_free(t, ("V", ev))
        v_qkv = sched.now("V")

        def f(v):
            v.tensor_scalar_mul(stats[:, RSTD1S:RSTD1S + 16],
                                stats[:, RSTD1:RSTD1 + 16], RSQRT2)
            return v.tensor_scalar_mul(stats[:, NEGMUR1S:NEGMUR1S + 16],
                                       stats[:, NEGMUR1:NEGMUR1 + 16], RSQRT2)
        v_qkv = EMIT("V", f, waits=[("V", v_qkv)], inc="V")

        bp_V = bp_S = 0
        if USE_BOUND:
            last_red = None
            for which, src_t in ((0, qF), (1, kF)):
                for hp in range(4):
                    def f(sc, hp=hp, src_t=src_t):
                        return sc.activation(P_A[:], src_t[:, hp, :], AF.Square)
                    s_sq2 = EMIT("S", f, waits=[("V", v_qkv)]
                                 + ([("V", last_red)] if last_red else []),
                                 inc="S")
                    for head in range(2):
                        hb = 64 * head
                        hidx = 2 * hp + head
                        colp = 2048 + 2048 * 0
                        def f(tn, hb=hb):
                            li = None
                            for nb in range(4):
                                li = tn.matmul(
                                    ps[0:1, 2048 + 512 * nb:2048 + 512 * (nb + 1)],
                                    ones_bf[hb:hb + 64, 256:257],
                                    P_A[hb:hb + 64, 512 * nb:512 * (nb + 1)],
                                    start=True, stop=True)
                            return li
                        t_nm = EMIT("T", f, waits=[("S", s_sq2)]
                                    + ([("V", last_red)] if last_red else []),
                                    inc="T")
                        def f(v, which=which, hidx=hidx):
                            return v.reduce_max(
                                rs_t[0:1, 32 + 16 * which + hidx:
                                     33 + 16 * which + hidx],
                                ps[0:1, 2048:4096], axis=AX)
                        last_red = EMIT("V", f, waits=[("T", t_nm)], inc="V")
            def f(v):
                return v.tensor_tensor(rs_t[0:1, 64:72], rs_t[0:1, 32:40],
                                       rs_t[0:1, 48:56], ALU.mult)
            v_bm = EMIT("V", f, waits=[("V", last_red)], inc="V")
            def f(sc):
                return sc.activation(rs_t[0:1, 64:72], rs_t[0:1, 64:72], AF.Sqrt)
            s_bq = EMIT("S", f, waits=[("V", v_bm)], inc="S")
            def f(v):
                return v.tensor_scalar_mul(rs_t[0:1, 64:72], rs_t[0:1, 64:72],
                                           -0.125)
            v_bb = EMIT("V", f, waits=[("S", s_bq)], inc="V")
            for hidx in range(8):
                def f(tn, hidx=hidx):
                    return tn.matmul(ps[0:128, 2048 + hidx:2048 + hidx + 1],
                                     ones_f[0:1, 0:128],
                                     rs_t[0:1, 64 + hidx:65 + hidx],
                                     start=(hidx == 0), stop=(hidx == 7))
                EMIT("T", f, waits=[("V", v_bb)],
                     inc="T" if hidx == 7 else None)
            t_bbc = sched.now("T")
            def f(v):
                return v.tensor_copy(rs_t[:, 16:24], ps[:, 2048:2056])
            v_bprep = EMIT("V", f, waits=[("T", t_bbc)], inc="V")
            bp_V, bp_S = v_bprep, sched.now("S")
        v_qkv = sched.now("V")

        steps = [(hp, qc, head) for hp in range(4) for qc in range(NT)
                 for head in range(2)]
        nsteps = len(steps)
        side_free = {0: [], 1: []}
        p_buf_free = {0: [], 1: []}
        pt_buf_free = {0: [], 1: []}
        st = {}
        pv_evac = None

        def b_head(i):
            hp, qc, head = steps[i]
            p = i % 2
            cb = 2048 * p
            P_t = P_A if p == 0 else P_B
            hb = 64 * head
            def f(tn, hp=hp, qc=qc, cb=cb, hb=hb):
                li = None
                for nb in range(4):
                    li = tn.matmul(ps[:, cb + 512 * nb:cb + 512 * (nb + 1)],
                                   qF[hb:hb + 64, hp, 128 * qc:128 * (qc + 1)],
                                   kF[hb:hb + 64, hp, 512 * nb:512 * (nb + 1)],
                                   start=True, stop=True, tile_position=(hb, 0))
                return li
            wl = list(side_free[p])
            if i == 0:
                wl.append(("V", v_qkv))
            if i <= 1 and USE_BOUND:
                wl.append(("V", bp_V))
            t_sim = EMIT("T", f, waits=wl, inc="T")
            if USE_BOUND:
                hidx = 2 * hp + head
                def f(sc, p=p, cb=cb, P_t=P_t, hidx=hidx):
                    return sc.activation(P_t[:], ps[:, cb:cb + 2048], AF.Exp,
                                         bias=rs_t[:, 16 + hidx:17 + hidx],
                                         scale=consts[:, CSCL:CSCL + 1],
                                         accum_out=rs_t[:, 4 + p:5 + p])
                s_exp = EMIT("S", f, waits=W("S", [("T", t_sim)])
                             + list(p_buf_free[p]), inc="S")
            else:
                def f(v, p=p, cb=cb):
                    return v.reduce_max(rs_t[:, p:p + 1], ps[:, cb:cb + 2048],
                                        axis=AX, negate=True)
                EMIT("V", f, waits=[("T", t_sim)], inc="V")
                def f(v, p=p):
                    return v.tensor_scalar_mul(rs_t[:, 2 + p:3 + p],
                                               rs_t[:, p:p + 1], 0.125)
                v_scale = EMIT("V", f, inc="V")
                def f(sc, p=p, cb=cb, P_t=P_t):
                    return sc.activation(P_t[:], ps[:, cb:cb + 2048], AF.Exp,
                                         bias=rs_t[:, 2 + p:3 + p],
                                         scale=consts[:, CSCL:CSCL + 1],
                                         accum_out=rs_t[:, 4 + p:5 + p])
                s_exp = EMIT("S", f, waits=W("S", [("V", v_scale)])
                             + list(p_buf_free[p]), inc="S")
            st[i] = {"exp": s_exp}

        def b_tail(i):
            hp, qc, head = steps[i]
            p = i % 2
            cb = 2048 * p
            P_t = P_A if p == 0 else P_B
            Pt_t = PtA if p == 0 else PtB
            hb = 64 * head
            def f(v, p=p):
                return v.reciprocal(rs_t[:, 6 + p:7 + p], rs_t[:, 4 + p:5 + p])
            EMIT("V", f, waits=[("S", st[i]["exp"])], inc="V")
            def f(v, p=p, P_t=P_t):
                return v.tensor_scalar_mul(P_t[:], P_t[:], rs_t[:, 6 + p:7 + p])
            v_norm = EMIT("V", f, inc="V")
            t_trq = []
            ptevq = []
            for q4 in range(4):
                def f(tn, cb=cb, P_t=P_t, q4=q4):
                    li = None
                    for j in range(4):
                        jj = 4 * q4 + j
                        col = cb + 128 * jj
                        li = tn.matmul(ps[:, col:col + 128],
                                       P_t[:, 128 * jj:128 * (jj + 1)],
                                       identB[:, 0:128],
                                       start=(j == 0), stop=(j == 3))
                    return li
                wl = [("V", v_norm)] if q4 == 0 else []
                t_trq.append(EMIT("T", f, waits=wl, inc="T"))
                def f(v, cb=cb, Pt_t=Pt_t, q4=q4):
                    return v.tensor_copy(Pt_t[:, 512 * q4:512 * (q4 + 1)],
                                         ps[:, cb + 512 * q4:cb + 512 * (q4 + 1)])
                ptevq.append(EMIT("V", f, waits=[("T", t_trq[q4])]
                                  + (list(pt_buf_free[p]) if q4 == 0 else []),
                                  inc="V"))
            def f(tn, hp=hp, head=head, cb=cb, hb=hb, Pt_t=Pt_t):
                li = None
                for kc in range(8):
                    li = tn.matmul(ps[hb:hb + 64, cb:cb + 128],
                                   v_tm[:, kc, 64 * (2 * hp + head):
                                        64 * (2 * hp + head) + 64],
                                   Pt_t[:, 128 * kc:128 * (kc + 1)],
                                   start=(kc == 0), stop=False,
                                   tile_position=(0, hb))
                return li
            EMIT("T", f, waits=[("V", ptevq[1])], inc="T")
            def f(tn, hp=hp, head=head, cb=cb, hb=hb, Pt_t=Pt_t):
                li = None
                for kc in range(8, NT):
                    li = tn.matmul(ps[hb:hb + 64, cb:cb + 128],
                                   v_tm[:, kc, 64 * (2 * hp + head):
                                        64 * (2 * hp + head) + 64],
                                   Pt_t[:, 128 * kc:128 * (kc + 1)],
                                   start=False, stop=(kc == NT - 1),
                                   tile_position=(0, hb))
                return li
            t_pv = EMIT("T", f, waits=[("V", ptevq[3])], inc="T")
            def f(v, hp=hp, qc=qc, cb=cb, hb=hb):
                return v.tensor_copy(oF[hb:hb + 64, hp, 128 * qc:128 * (qc + 1)],
                                     ps[hb:hb + 64, cb:cb + 128])
            oev = EMIT("V", f, waits=[("T", t_pv)], inc="V")
            side_free[p] = [("V", oev)]
            p_buf_free[p] = [("T", t_trq[3])]
            pt_buf_free[p] = [("T", t_pv)]
            return oev

        b_head(0)
        for i in range(1, nsteps):
            b_head(i)
            pv_evac = b_tail(i - 1)
        pv_evac = b_tail(nsteps - 1)
        v_attn = pv_evac

        cm = {}
        czv = {}
        c2v = {}

        def c_S1(t):
            base = B0 if t % 2 == 0 else B2
            wl = p0r.waits(t)
            def f(tn, t=t, base=base):
                li = None
                for vc in range(4):
                    for nh in range(2):
                        li = tn.matmul(ps[:, base + 512 * nh:base + 512 * (nh + 1)],
                                       oF[:, vc, 128 * t:128 * (t + 1)],
                                       wout_sb[:, vc, 512 * nh:512 * (nh + 1)],
                                       start=(vc == 0), stop=(vc == 3))
                return li
            extra = [("V", v_attn)] if t == 0 else []
            t_z = EMIT("T", f, waits=wl + extra, inc="T")
            wz = z1r.waits(t)
            def f(v, t=t, base=base):
                return v.scalar_tensor_tensor(
                    z1_ring[:, t % 2, :], ps[:, base:base + 1024], 0.0,
                    boutB[:, :], ALU.bypass, ALU.add,
                    accum_out=stats[:, MUZ + t:MUZ + t + 1])
            v_z1 = EMIT("V", f, waits=wz + [("T", t_z)], inc="V")
            p0r.set_free(t, ("V", v_z1))
            def f(sc, t=t):
                return sc.activation(sq_dump[:, :], z1_ring[:, t % 2, :],
                                     AF.Square,
                                     accum_out=stats[:, RSTDZ + t:RSTDZ + t + 1])
            s_sqz = EMIT("S", f, waits=[("V", v_z1)], inc="S")
            x_re = EMIT("SY", dma_sy(x_ring[:, t % 2, :], x_r3[:, s * NT + t, :]),
                        waits=xr.waits(t), inc="SA")
            cm[t] = {"sqz": s_sqz, "xre": x_re}

        def c_chain(pair, MU, RSTD, NEGMUR, s_last):
            a = 2 * pair
            MUs = stats[:, MU + a:MU + a + 2]
            RSs = stats[:, RSTD + a:RSTD + a + 2]
            NMs = stats[:, NEGMUR + a:NEGMUR + a + 2]
            EMIT("V", lambda v: v.tensor_scalar_mul(MUs, MUs, 1.0 / C),
                 waits=[("S", s_last)], inc="V")
            EMIT("V", lambda v: v.scalar_tensor_tensor(NMs, MUs, 1.0, MUs,
                                                       ALU.mult, ALU.mult),
                 inc="V")
            v_c = EMIT("V", lambda v: v.scalar_tensor_tensor(
                RSs, RSs, 1.0 / C, NMs, ALU.mult, ALU.subtract), inc="V")
            s_d = EMIT("S", lambda sc: sc.activation(
                RSs, RSs, AF.Sqrt, bias=consts[:, CEPS:CEPS + 1]),
                waits=[("V", v_c)], inc="S")
            EMIT("V", lambda v: v.reciprocal(RSs, RSs),
                 waits=[("S", s_d)], inc="V")
            return EMIT("V", lambda v: v.scalar_tensor_tensor(
                NMs, MUs, -1.0, RSs, ALU.mult, ALU.mult), inc="V")

        def c_S2(t):
            pair = t // 2
            def f(sc, t=t):
                return sc.activation(scr1[:, t % 2, :], z1_ring[:, t % 2, :],
                                     AF.Identity,
                                     bias=stats[:, NEGMURZ + t:NEGMURZ + t + 1],
                                     scale=stats[:, RSTDZ + t:RSTDZ + t + 1])
            s_zn = EMIT("S", f, waits=scr1r.waits(t) + [("V", czv[pair])],
                        inc="S")
            z1r.set_free(t, ("S", s_zn))
            def f(g, t=t):
                return g.tensor_tensor(scr1[:, t % 2, :], scr1[:, t % 2, :],
                                       gamB[:, :], ALU.mult)
            g_y0 = EMIT("G", f, waits=W("G", [("S", s_zn)]), inc="G")
            def f(g, t=t):
                return g.tensor_tensor(scr1[:, t % 2, :], scr1[:, t % 2, :],
                                       betB[:, :], ALU.add)
            g_y = EMIT("G", f, waits=[("G", g_y0)], inc="G")
            wx = x1wr.waits(t)
            def f(v, t=t):
                return v.tensor_scalar(x1w[:, t % 2, :], x_ring[:, t % 2, :],
                                       stats[:, RSTD1S + t:RSTD1S + t + 1],
                                       stats[:, NEGMUR1S + t:NEGMUR1S + t + 1],
                                       ALU.mult, ALU.add)
            v_l1 = EMIT("V", f, waits=wx + [("SA", cm[t]["xre"])], inc="V")
            def f(v, t=t):
                return v.tensor_tensor(x1w[:, t % 2, :], x1w[:, t % 2, :],
                                       scr1[:, t % 2, :], ALU.add)
            v_t5 = EMIT("V", f, waits=[("G", g_y)], inc="V")
            scr1r.set_free(t, ("V", v_t5))
            def f(v, t=t):
                return v.scalar_tensor_tensor(
                    x1w[:, t % 2, :], x1w[:, t % 2, :], 1.0,
                    x_ring[:, t % 2, :], ALU.bypass, ALU.add,
                    accum_out=stats[:, MU2 + t:MU2 + t + 1])
            v_x1 = EMIT("V", f, inc="V")
            xr.set_free(t, ("V", v_x1))
            st_x1 = EMIT("SY", dma_sy(x1_r3[:, t, :], x1w[:, t % 2, :]),
                         waits=[("V", v_x1)], inc="SA2")
            def f(sc, t=t):
                return sc.activation(sq_dump[:, :], x1w[:, t % 2, :], AF.Square,
                                     accum_out=stats[:, RSTD2 + t:RSTD2 + t + 1])
            s_sq1 = EMIT("S", f, waits=[("V", v_x1)], inc="S")
            cm[t]["sq1"] = s_sq1
            cm[t]["stx1"] = st_x1

        def c_S3(t):
            pair = t // 2
            def f(sc, t=t):
                return sc.activation(lnx_ring[:, t % 2, :], x1w[:, t % 2, :],
                                     AF.Identity,
                                     bias=stats[:, NEGMUR2 + t:NEGMUR2 + t + 1],
                                     scale=stats[:, RSTD2 + t:RSTD2 + t + 1])
            s_ln2 = EMIT("S", f, waits=lr.waits(t) + [("V", c2v[pair])], inc="S")
            x1wr.set_free(t, ("S", s_ln2), ("SA2", cm[t]["stx1"]))
            base2 = TP0 if t % 2 == 0 else TP1
            def f(tn, t=t, base2=base2):
                li = None
                for c in range(KC):
                    li = tn.matmul(ps[:, base2 + 128 * c:base2 + 128 * (c + 1)],
                                   lnx_ring[:, t % 2, 128 * c:128 * (c + 1)],
                                   identB[:, 0:128],
                                   start=(c % 4 == 0), stop=(c % 4 == 3))
                return li
            t_tr2 = EMIT("T", f, waits=tpr.waits(t) + [("S", s_ln2)], inc="T")
            lr.set_free(t, ("T", t_tr2))
            def f(v, t=t, base2=base2):
                v.tensor_copy(lnF[:, 0:4, 128 * t:128 * (t + 1)],
                              ps[:, base2:base2 + 512].rearrange(
                                  "p (a b) -> p a b", b=128))
                return v.tensor_copy(lnF[:, 4:8, 128 * t:128 * (t + 1)],
                                     ps[:, base2 + 512:base2 + 1024].rearrange(
                                         "p (a b) -> p a b", b=128))
            v_ln2F = EMIT("V", f, waits=[("T", t_tr2)], inc="V")
            tpr.set_free(t, ("V", v_ln2F))
            return v_ln2F

        for kk in range(10):
            if kk >= 2:
                v_ln2F = c_S3(2 * kk - 4)
                v_ln2F = c_S3(2 * kk - 3)
            if 1 <= kk <= 8:
                c_S2(2 * kk - 2)
                c_S2(2 * kk - 1)
            if kk <= 7:
                c_S1(2 * kk)
                c_S1(2 * kk + 1)
                czv[kk] = c_chain(kk, MUZ, RSTDZ, NEGMURZ, cm[2 * kk + 1]["sqz"])
            if 1 <= kk <= 8:
                c2v[kk - 1] = c_chain(kk - 1, MU2, RSTD2, NEGMUR2,
                                      cm[2 * kk - 1]["sq1"])
        s_C_end = sched.now("S")
        v_C_end = sched.now("V")
        g_C_end = sched.now("G")
        sa_C_end = sched.now("SA")
        sa2_C_end = sched.now("SA2")

        def f(tn):
            li = None
            for nb in range(4):
                for kc in range(KC):
                    li = tn.matmul(ps[0:16, 2048 + 512 * nb:2048 + 512 * (nb + 1)],
                                   rwT[:, kc, 0:16],
                                   lnF[:, kc, 512 * nb:512 * (nb + 1)],
                                   start=(kc == 0), stop=(kc == KC - 1))
            return li
        t_lg = EMIT("T", f, waits=[("V", v_ln2F)], inc="T")

        def f(v):
            return v.tensor_copy(lgF[0:16, :], ps[0:16, 2048:4096])
        v_lg = EMIT("V", f, waits=[("T", t_lg)], inc="V")

        def f(tn):
            li = None
            for t in range(NT):
                li = tn.matmul(ps[0:128, 2048 + 16 * t:2048 + 16 * (t + 1)],
                               lgF[0:16, 128 * t:128 * (t + 1)],
                               identF[0:16, 0:16],
                               start=(t == 0), stop=(t == NT - 1))
            return li
        t_lgt = EMIT("T", f, waits=[("V", v_lg)], inc="T")

        def f(sc):
            return sc.activation(r_s[:, :], ps[:, 2048:2304], AF.Sigmoid)
        s_sig = EMIT("S", f, waits=[("T", t_lgt)], inc="S")

        def vop(fn, waits=()):
            return EMIT("V", fn, waits=list(waits), inc="V")

        sb3v = r_sb[:, :].rearrange("p (t e) -> p t e", e=16)
        s3v = r_s[:, :].rearrange("p (t e) -> p t e", e=16)
        cb3v = cbB[:, 0:16].rearrange("p (o e) -> p o e", o=1).broadcast_to(
            [128, 16, 16])
        sb4v = r_sb[:, :].rearrange("p (t g j) -> p t g j", g=4, j=4)
        p6v = r_scr[:, 0:384].rearrange("p (t g x) -> p t g x", g=4, x=6)
        gs3v = r_gs[:, 0:64].rearrange("p (t g) -> p t g", g=4)
        cg3v = r_gs[:, 64:128].rearrange("p (t g) -> p t g", g=4)
        tmp3v = r_gs[:, 128:192].rearrange("p (t g) -> p t g", g=4)
        emEv = r_gs[:, 64:128].rearrange("p (t g o) -> p t g o", g=4, o=1) \
            .broadcast_to([128, 16, 4, 4])
        m4v = r_msk[:, :].rearrange("p (t g j) -> p t g j", g=4, j=4)
        m3v = r_msk[:, :].rearrange("p (t e) -> p t e", e=16)
        c3v = r_c16[:, :].rearrange("p (t e) -> p t e", e=16)
        w3v = r_w[:, :].rearrange("p (t e) -> p t e", e=16)
        tAv = r_scr[:, 0:256].rearrange("p (t e) -> p t e", e=16)
        tBv = r_scr[:, 256:512].rearrange("p (t e) -> p t e", e=16)

        vop(lambda v: v.tensor_tensor(sb3v, s3v, cb3v, ALU.add),
            waits=[("S", s_sig)])
        pairs = [(0, 1), (0, 2), (0, 3), (1, 2), (1, 3), (2, 3)]
        def f(v):
            li = None
            for idx, (i, j) in enumerate(pairs):
                li = v.tensor_tensor(p6v[:, :, :, idx], sb4v[:, :, :, i],
                                     sb4v[:, :, :, j], ALU.add)
            return li
        vop(f)
        vop(lambda v: v.tensor_reduce(gs3v, p6v, axis=AX, op=ALU.max))
        vop(lambda v: v.memset(cg3v, 0.0))
        for d in (1, 2, 3):
            def f(v, d=d):
                v.tensor_tensor(tmp3v[:, :, 0:4 - d], gs3v[:, :, d:4],
                                gs3v[:, :, 0:4 - d], ALU.is_gt)
                return v.tensor_tensor(tAv[:, :, 0:4 - d], gs3v[:, :, 0:4 - d],
                                       gs3v[:, :, d:4], ALU.is_gt)
            vop(f)
            vop(lambda v, d=d: v.tensor_tensor(cg3v[:, :, 0:4 - d],
                                               cg3v[:, :, 0:4 - d],
                                               tmp3v[:, :, 0:4 - d], ALU.add))
            vop(lambda v, d=d: v.tensor_tensor(cg3v[:, :, d:4], cg3v[:, :, d:4],
                                               tAv[:, :, 0:4 - d], ALU.add))
        vop(lambda v: v.tensor_single_scalar(cg3v, cg3v, 1.5, ALU.is_lt))
        vop(lambda v: v.tensor_tensor(m4v, sb4v, emEv, ALU.mult))
        vop(lambda v: v.tensor_scalar(tAv, emEv.rearrange("p a g o -> p (a g o)")
                                      if False else
                                      r_gs[:, 64:128].rearrange(
                                          "p (t g o) -> p t g o", g=4, o=1)
                                      .broadcast_to([128, 16, 4, 4]),
                                      1e30, 1e30, ALU.mult, ALU.subtract))
        vop(lambda v: v.tensor_tensor(m3v, m3v,
                                      tAv.rearrange("p a e -> p (a e)")
                                      if False else r_scr[:, 0:256]
                                      .rearrange("p (t e) -> p t e", e=16),
                                      ALU.add))
        vop(lambda v: v.memset(c3v, 0.0))
        for d in range(1, 16):
            def f(v, d=d):
                v.tensor_tensor(tAv[:, :, 0:16 - d], m3v[:, :, d:16],
                                m3v[:, :, 0:16 - d], ALU.is_gt)
                return v.tensor_tensor(tBv[:, :, 0:16 - d], m3v[:, :, 0:16 - d],
                                       m3v[:, :, d:16], ALU.is_gt)
            vop(f)
            vop(lambda v, d=d: v.tensor_tensor(c3v[:, :, 0:16 - d],
                                               c3v[:, :, 0:16 - d],
                                               tAv[:, :, 0:16 - d], ALU.add))
            vop(lambda v, d=d: v.tensor_tensor(c3v[:, :, d:16], c3v[:, :, d:16],
                                               tBv[:, :, 0:16 - d], ALU.add))
        vop(lambda v: v.tensor_single_scalar(r_c16[:, :], r_c16[:, :], 3.5,
                                             ALU.is_lt))
        vop(lambda v: v.tensor_tensor(r_w[:, :], r_s[:, :], r_c16[:, :],
                                      ALU.mult))
        vop(lambda v: v.tensor_reduce(r_ws[:, 0:16], w3v, axis=AX, op=ALU.add))
        vop(lambda v: v.tensor_scalar_add(r_ws[:, 0:16], r_ws[:, 0:16], 1e-20))
        vop(lambda v: v.reciprocal(r_ws[:, 16:32], r_ws[:, 0:16]))
        wr3v = r_ws[:, 16:32].rearrange("p (e o) -> p e o", o=1).broadcast_to(
            [128, 16, 16])
        ct3v = comb_tm[:, 0:256].rearrange("p (t e) -> p t e", e=16)
        v_route = vop(lambda v: v.tensor_tensor(ct3v, w3v, wr3v, ALU.mult))

        def f(tn):
            li = None
            for t in range(NT):
                li = tn.matmul(ps[0:16, 2048 + 128 * t:2048 + 128 * (t + 1)],
                               comb_tm[:, 16 * t:16 * (t + 1)],
                               identB[:, 0:128],
                               start=(t % 4 == 0), stop=(t % 4 == 3))
            return li
        t_cb = EMIT("T", f, waits=[("V", v_route)], inc="T")

        def f(v):
            return v.tensor_copy(combF[0:16, :], ps[0:16, 2048:4096])
        v_combF = EMIT("V", f, waits=[("T", t_cb)], inc="V")

        sg_r = sgw.rearrange("(kc p) hh -> p kc hh", p=128)
        su_r = suw.rearrange("(kc p) hh -> p kc hh", p=128)
        hst_free = {0: (), 1: ()}
        sil_free = {0: (), 1: ()}
        pp_free = {0: [], 1: []}
        uidx = 0
        t_gu = None
        for hc in range(SHID // 128):
            wl = swr.waits(hc)
            if hc < 2:
                wl += [("S", s_C_end), ("V", v_C_end)]
            ldg = EMIT("G", dma_g(swg_ring[:, hc % 4, :, :],
                                  sg_r[:, :, 128 * hc:128 * (hc + 1)]),
                       waits=wl, inc="SG")
            ldu = EMIT("G", dma_g(swu_ring[:, hc % 4, :, :],
                                  su_r[:, :, 128 * hc:128 * (hc + 1)]),
                       inc="SG2")
            ev_last = None
            for tb in range(4):
                par = uidx % 2
                gc = 512 * par
                uc = 1024 + 512 * par
                def f(tn, hc=hc, tb=tb, gc=gc, uc=uc):
                    li = None
                    for kc in range(KC):
                        tn.matmul(ps[:, gc:gc + 512], swg_ring[:, hc % 4, kc, :],
                                  lnF[:, kc, 512 * tb:512 * (tb + 1)],
                                  start=(kc == 0), stop=(kc == KC - 1))
                    for kc in range(KC):
                        li = tn.matmul(ps[:, uc:uc + 512],
                                       swu_ring[:, hc % 4, kc, :],
                                       lnF[:, kc, 512 * tb:512 * (tb + 1)],
                                       start=(kc == 0), stop=(kc == KC - 1))
                    return li
                wl = [("SG", ldg), ("SG2", ldu)] + list(pp_free[par])
                if tb == 0:
                    wl += list(hst_free[hc % 2])
                    if hc == 0:
                        wl += [("SA", sa_C_end), ("SA2", sa2_C_end)]
                t_gu = EMIT("T", f, waits=wl, inc="T")

                def f(sc, par=par):
                    return sc.activation(silu_scr[:, par, :],
                                         ps[:, 512 * par:512 * par + 512],
                                         AF.Silu)
                s_sg = EMIT("S", f, waits=[("T", t_gu)] + list(sil_free[par]),
                            inc="S")

                def f(v, hc=hc, tb=tb, par=par, uc=uc):
                    return v.tensor_tensor(
                        h_stage[:, hc % 2, 512 * tb:512 * (tb + 1)],
                        silu_scr[:, par, :], ps[:, uc:uc + 512], ALU.mult)
                ev_last = EMIT("V", f, waits=[("S", s_sg)], inc="V")
                pp_free[par] = [("V", ev_last)]
                sil_free[par] = (("V", ev_last),)
                uidx += 1
            swr.set_free(hc, ("T", t_gu))
            st_h = EMIT("SY", dma_sy(h_r3[:, hc, :], h_stage[:, hc % 2, :]),
                        waits=[("V", ev_last)], inc="SA")
            hst_free[hc % 2] = (("SA", st_h),)
        t_shared_end = t_gu

        v_cb = None
        cb_free = {0: (("V", v_combF),), 1: (("V", v_combF),)}
        for e in range(E):
            cb_t = cB_sb if e % 2 == 0 else cb2_sb
            wl = list(cb_free[e % 2])
            if e < 2:
                wl.append(("V", v_C_end))
            sa_cst = EMIT("G", dma_g(cb_t[0:1, :], combF[e:e + 1, :]),
                          waits=wl, inc="SD")
            wl = wgr.waits(e)
            if e < 2:
                wl += [("T", t_shared_end), ("V", v_C_end), ("G", g_C_end)]
            ldg = EMIT("G", dma_g(wg_buf[:, e % 2, :, :],
                                  gw[e].rearrange("(kc p) hh -> p kc hh", p=128)),
                       waits=wl, inc="SG")
            ldu = EMIT("G", dma_g(wu_buf[:, e % 2, :, :],
                                  uw[e].rearrange("(kc p) hh -> p kc hh", p=128)),
                       inc="SG2")
            sg_wd = EMIT("G", dma_g(wd_sb[:, 16 + 2 * e:18 + 2 * e, :],
                                    dw[e].rearrange("(kc p) f -> p kc f", p=128)),
                         waits=[("T", t_lgt)] if e == 0 else [], inc="SG")
            if e < 4:
                sg_wd = EMIT("G", dma_g(
                    wd_sb[:, 4 * e:4 * (e + 1), :],
                    sdw.rearrange("(kc p) f -> p kc f", p=128)[:, 4 * e:4 * (e + 1), :]),
                    inc="SG")

            def f(tn, e=e, cb_t=cb_t):
                li = None
                for nb in range(4):
                    li = tn.matmul(ps[:, 2048 + 512 * nb:2048 + 512 * (nb + 1)],
                                   ones_bf[0:1, 0:128],
                                   cb_t[0:1, 512 * nb:512 * (nb + 1)],
                                   start=True, stop=True)
                return li
            t_cbb = EMIT("T", f,
                         waits=[("SD", sa_cst)]
                         + ([("V", v_cb)] if v_cb else []), inc="T")

            def f(v, cb_t=cb_t):
                return v.tensor_copy(cb_t[:, :], ps[:, 2048:4096])
            v_cb = EMIT("V", f, waits=[("T", t_cbb)], inc="V")

            for hc in range(2):
                ev_last = None
                for tb in range(4):
                    par = uidx % 2
                    gc = 512 * par
                    uc = 1024 + 512 * par
                    usc = 1024 + 512 * par
                    def f(tn, e=e, hc=hc, tb=tb, gc=gc, uc=uc):
                        li = None
                        for kc in range(KC):
                            tn.matmul(ps[:, gc:gc + 512],
                                      wg_buf[:, e % 2, kc, 128 * hc:128 * (hc + 1)],
                                      lnF[:, kc, 512 * tb:512 * (tb + 1)],
                                      start=(kc == 0), stop=(kc == KC - 1))
                        for kc in range(KC):
                            li = tn.matmul(ps[:, uc:uc + 512],
                                           wu_buf[:, e % 2, kc,
                                                  128 * hc:128 * (hc + 1)],
                                           lnF[:, kc, 512 * tb:512 * (tb + 1)],
                                           start=(kc == 0), stop=(kc == KC - 1))
                        return li
                    wl = [("SG", ldg), ("SG2", ldu)] + list(pp_free[par])
                    if tb == 0:
                        wl += list(hst_free[hc % 2])
                        if hc == 0:
                            wl.append(("V", v_cb))
                    t_gu = EMIT("T", f, waits=wl, inc="T")

                    def f(sc, par=par):
                        return sc.activation(silu_scr[:, par, :],
                                             ps[:, 512 * par:512 * par + 512],
                                             AF.Silu)
                    s_sg = EMIT("S", f,
                                waits=[("T", t_gu)] + list(sil_free[par]),
                                inc="S")

                    def f(v, hc=hc, tb=tb, uc=uc, cb_t=cb_t):
                        return v.tensor_tensor(
                            h_stage[:, hc % 2, 512 * tb:512 * (tb + 1)],
                            ps[:, uc:uc + 512],
                            cb_t[:, 512 * tb:512 * (tb + 1)], ALU.mult)
                    ev_m = EMIT("V", f, waits=[("S", s_sg)], inc="V")

                    def f(v, hc=hc, tb=tb, par=par):
                        return v.tensor_tensor(
                            h_stage[:, hc % 2, 512 * tb:512 * (tb + 1)],
                            h_stage[:, hc % 2, 512 * tb:512 * (tb + 1)],
                            silu_scr[:, par, :], ALU.mult)
                    ev_last = EMIT("V", f, waits=[("V", ev_m)], inc="V")
                    pp_free[par] = [("S", s_sg), ("V", ev_m)]
                    sil_free[par] = (("V", ev_last),)
                    uidx += 1
                st_h = EMIT("SY", dma_sy(h_r3[:, 16 + 2 * e + hc, :],
                                         h_stage[:, hc % 2, :]),
                            waits=[("V", ev_last)], inc="SA")
                hst_free[hc % 2] = (("SA", st_h),)
            cb_free[e % 2] = (("V", ev_m),)
            wgr.set_free(e, ("T", t_gu))
        t_gu_end = t_gu
        sa_h_done = sched.now("SA")
        s_D_end = sched.now("S")
        v_D_end = sched.now("V")


        for t in range(NT):
            wl = hr.waits(t)
            if t < 2:
                wl += [("T", t_gu_end), ("SA", sa_h_done)]
            ld_h = EMIT("S", dma_sy(h_ring[:, t % 2, :, :],
                                    h_r3[:, :, 128 * t:128 * (t + 1)]),
                        waits=wl, inc="SC" if t % 2 == 0 else "SC2")
            wl = x1er.waits(t)
            if t < 2:
                wl += [("T", t_gu_end)]
            ld_x1 = EMIT("SY", dma_sy(x1e_ring[:, t % 2, :], x1_r3[:, t, :]),
                         waits=wl, inc="SA")
            base = B0 if t % 2 == 0 else B2
            wl = p0r.waits(t)
            if t < 2:
                wl += [("S", s_D_end), ("V", v_D_end)]
            def f(tn, t=t, base=base):
                li = None
                for kc in range(NKC_H):
                    for nh in range(2):
                        li = tn.matmul(ps[:, base + 512 * nh:base + 512 * (nh + 1)],
                                       h_ring[:, t % 2, kc, :],
                                       wd_sb[:, kc, 512 * nh:512 * (nh + 1)],
                                       start=(kc == 0), stop=(kc == NKC_H - 1))
                return li
            t_d = EMIT("T", f, waits=wl + [(("SC" if t % 2 == 0 else "SC2"), ld_h), ("SG", sg_wd)], inc="T")
            hr.set_free(t, ("T", t_d))

            def f(v, t=t, base=base):
                return v.scalar_tensor_tensor(x2_ring[:, t % 2, :],
                                              ps[:, base:base + 1024], 0.0,
                                              x1e_ring[:, t % 2, :],
                                              ALU.bypass, ALU.add)
            v_x2 = EMIT("V", f, waits=x2r.waits(t) + [("T", t_d), ("SA", ld_x1)],
                        inc="V")
            p0r.set_free(t, ("V", v_x2))
            x1er.set_free(t, ("V", v_x2))

            st = EMIT("SY", dma_sy(y_r3[:, s * NT + t, :], x2_ring[:, t % 2, :]),
                      waits=[("V", v_x2)], inc="SA2")
            x2r.set_free(t, ("SA2", st))

        if s < n_seq - 1:
            t_end_E = sched.now("T")
            EMIT("G", dma_g(wqk_sb[:], wqk.rearrange("(kc p) f -> p kc f", p=128)),
                 waits=[("T", t_end_E)], inc="SG")
            EMIT("G", dma_g(wv_sb[:], wv.rearrange("(kc p) f -> p kc f", p=128)),
                 inc="SG")
            sg_attnw = EMIT("G", dma_g(wout_sb[:],
                                       wout.rearrange("(kc p) f -> p kc f", p=128)),
                            inc="SG")

        endbar = sched.bar()
        deps = tuple((k, endbar[k])
                     for k in ("T", "V", "S", "SA", "SC", "SA2", "SC2"))
        for r in (lr, tpr, p0r, z1r, hr, x1er, x2r, swr, wgr, x1wr, scr1r):
            r.reset(*deps)
        xr.reset(cb_free[0][0], cb_free[1][0])

    sa_final = sched.now("SA")
    sa2_final = sched.now("SA2")

    def f(g):
        return g.nop()
    EMIT("G", f, waits=[("SA", sa_final), ("SA2", sa2_final)])

    with (
        nc.semaphore("semT") as semT,
        nc.semaphore("semV") as semV,
        nc.semaphore("semS") as semS,
        nc.semaphore("semG") as semG,
        nc.semaphore("semSA") as semSA,
        nc.semaphore("semSG") as semSG,
        nc.semaphore("semSC") as semSC,
        nc.semaphore("semSA2") as semSA2,
        nc.semaphore("semSG2") as semSG2,
        nc.semaphore("semSC2") as semSC2,
        nc.semaphore("semSD") as semSD,
        nc.Block() as blk,
    ):
        sems = {"T": semT, "V": semV, "S": semS, "G": semG, "SA": semSA,
                "SG": semSG, "SC": semSC, "SA2": semSA2, "SG2": semSG2,
                "SC2": semSC2, "SD": semSD}

        @blk.tensor
        def _(eng):
            _emit_engine(sched, "T", eng, sems)

        @blk.vector
        def _(eng):
            _emit_engine(sched, "V", eng, sems)

        @blk.scalar
        def _(eng):
            _emit_engine(sched, "S", eng, sems)

        @blk.gpsimd
        def _(eng):
            _emit_engine(sched, "G", eng, sems)

        @blk.sync
        def _(eng):
            _emit_engine(sched, "SY", eng, sems)

    return nc


def _host_prep(inputs):
    W_qk = np.asarray(inputs["L0_attn_W_qk"])
    perm = np.empty(1024, dtype=np.int64)
    for h in range(H):
        for d in range(DH):
            perm[h * DH + d] = h * 128 + 2 * d
            perm[512 + h * DH + d] = h * 128 + 2 * d + 1
    return {
        "wqk": np.ascontiguousarray(W_qk[:, perm]),
        "bqk": np.ascontiguousarray(np.asarray(inputs["L0_attn_b_qk"])[perm])[None, :],
        "wv": np.ascontiguousarray(inputs["L0_attn_W_v"]),
        "bv": np.asarray(inputs["L0_attn_b_v"])[None, :],
        "wout": np.ascontiguousarray(inputs["L0_attn_W_out"]),
        "bout": np.asarray(inputs["L0_attn_b_out"])[None, :],
        "gam": np.asarray(inputs["L0_attn_ln_gamma"])[None, :],
        "bet": np.asarray(inputs["L0_attn_ln_beta"])[None, :],
        "rw": np.ascontiguousarray(inputs["L0_ffn_router_w"]),
        "cb": np.asarray(inputs["L0_ffn_correction_bias"])[None, :],
        "gw": np.ascontiguousarray(inputs["L0_ffn_gate_w"]),
        "uw": np.ascontiguousarray(inputs["L0_ffn_up_w"]),
        "dw": np.ascontiguousarray(inputs["L0_ffn_down_w"]),
        "sgw": np.ascontiguousarray(np.asarray(inputs["L0_ffn_shared_gate_w"]).T),
        "suw": np.ascontiguousarray(np.asarray(inputs["L0_ffn_shared_up_w"]).T),
        "sdw": np.ascontiguousarray(np.asarray(inputs["L0_ffn_shared_down_w"]).T),
    }


def kernel(**inputs):
    x = np.asarray(inputs["L0_attn_x"])
    com = _host_prep(inputs)
    nc = build_nc(NSEQ)
    in_maps = []
    for c in range(NCORES):
        m = dict(com)
        m["x"] = np.ascontiguousarray(
            x[c * NSEQ:(c + 1) * NSEQ].reshape(NSEQ * T, C))
        in_maps.append(m)
    res = run_bass_kernel_spmd(nc, in_maps, core_ids=list(range(NCORES)))
    out = np.empty((NCORES * NSEQ, T, C), dtype=np.float32)
    for c in range(NCORES):
        out[c * NSEQ:(c + 1) * NSEQ] = np.asarray(
            res.results[c]["y"], dtype=np.float32).reshape(NSEQ, T, C)
    return out


# version 18
# speedup vs baseline: 1.0386x; 1.0386x over previous
import numpy as np
import concourse.bass as bass
import concourse.mybir as mybir
from concourse.bass_utils import run_bass_kernel_spmd

F32 = mybir.dt.float32
BF16 = mybir.dt.bfloat16
U8 = mybir.dt.uint8
AF = mybir.ActivationFunctionType
ALU = mybir.AluOpType
AX = mybir.AxisListType.X

NCORES = 8
NSEQ = 4
T = 2048
C = 1024
NT = T // 128
H = 8
DH = 64
E = 16
EH = 256
SHID = 2048
KC = C // 128
NKC_H = (SHID + E * EH) // 128
RSQRT2 = float(2.0 ** -0.5)
USE_BOUND = True
LN_EPS = 1e-5
KB = 1024


class Sched:
    ENGINES = ("T", "V", "S", "G", "SY")

    def __init__(self):
        self.ops = {e: [] for e in self.ENGINES}
        self.cnt = {"T": 0, "V": 0, "S": 0, "G": 0, "SA": 0, "SG": 0,
                    "SC": 0, "SA2": 0, "SG2": 0, "SC2": 0, "SD": 0}

    def emit(self, eng, fn, waits=(), inc=None):
        waits = [(s, v) for (s, v) in waits if v and v > 0]
        for s, v in waits:
            assert v <= self.cnt[s], (s, v, self.cnt[s])
        after = None
        if inc in ("SA", "SG", "SC", "SA2", "SG2", "SC2", "SD"):
            if self.cnt[inc] > 0:
                waits = [(inc, self.cnt[inc])] + waits
        elif inc in ("V", "S", "G"):
            if self.cnt[inc] > 0:
                waits = [(inc, self.cnt[inc])] + waits
        if inc is not None:
            self.cnt[inc] += 16 if inc not in ("T", "V", "S", "G") else 1
            after = self.cnt[inc]
        self.ops[eng].append((waits, fn, inc, after))
        return after

    def now(self, sem):
        return self.cnt[sem]

    def bar(self):
        return dict(self.cnt)


def _emit_engine(sched, eng_name, eng, sems):
    last_wait = {}
    for waits, fn, inc, after in sched.ops[eng_name]:
        for s, v in waits:
            if last_wait.get(s, -1) >= v:
                continue
            eng.wait_ge(sems[s], v)
            last_wait[s] = v
        inst = fn(eng)
        if inc is not None:
            assert inst is not None
            inst.then_inc(sems[inc],
                          16 if inc not in ("T", "V", "S", "G") else 1)


class Ring:
    def __init__(self, n):
        self.free = [()] * n
        self.n = n

    def waits(self, i):
        return list(self.free[i % self.n])

    def set_free(self, i, *deps):
        self.free[i % self.n] = deps

    def reset(self, *deps):
        self.free = [deps] * self.n


def build_nc(n_seq=NSEQ):
    nc = bass.Bass("TRN2", target_bir_lowering=False)

    x_in = nc.dram_tensor("x", [n_seq * T, C], F32, kind="ExternalInput").ap()
    wqk = nc.dram_tensor("wqk", [C, 1024], F32, kind="ExternalInput").ap()
    bqk = nc.dram_tensor("bqk", [1, 1024], F32, kind="ExternalInput").ap()
    wv = nc.dram_tensor("wv", [C, 512], F32, kind="ExternalInput").ap()
    bv = nc.dram_tensor("bv", [1, 512], F32, kind="ExternalInput").ap()
    wout = nc.dram_tensor("wout", [512, C], F32, kind="ExternalInput").ap()
    bout = nc.dram_tensor("bout", [1, C], F32, kind="ExternalInput").ap()
    gam = nc.dram_tensor("gam", [1, C], F32, kind="ExternalInput").ap()
    bet = nc.dram_tensor("bet", [1, C], F32, kind="ExternalInput").ap()
    rw = nc.dram_tensor("rw", [16, C], F32, kind="ExternalInput").ap()
    cb = nc.dram_tensor("cb", [1, 16], F32, kind="ExternalInput").ap()
    gw = nc.dram_tensor("gw", [E, C, EH], F32, kind="ExternalInput").ap()
    uw = nc.dram_tensor("uw", [E, C, EH], F32, kind="ExternalInput").ap()
    dw = nc.dram_tensor("dw", [E, EH, C], F32, kind="ExternalInput").ap()
    sgw = nc.dram_tensor("sgw", [C, SHID], F32, kind="ExternalInput").ap()
    suw = nc.dram_tensor("suw", [C, SHID], F32, kind="ExternalInput").ap()
    sdw = nc.dram_tensor("sdw", [SHID, C], F32, kind="ExternalInput").ap()
    y_out = nc.dram_tensor("y", [n_seq * T, C], F32, kind="ExternalOutput").ap()
    h_cache = nc.dram_tensor("h_cache", [NKC_H * 128, T], BF16, kind="Internal").ap()
    x1_cache = nc.dram_tensor("x1_cache", [T, C], F32, kind="Internal").ap()

    ARENA_KB = 204
    arena_h = nc.alloc_sbuf_tensor("arena", [128, ARENA_KB * KB], U8)

    def reg(off_kb, size_kb, dtype, shape):
        a = arena_h[:, off_kb * KB:(off_kb + size_kb) * KB].bitcast(dtype)
        n = a.shape[1]
        want = 1
        for sdim in shape[1:]:
            want *= sdim
        assert want == n, (shape, n, off_kb, size_kb)
        if len(shape) == 2:
            return a
        if len(shape) == 3:
            return a.rearrange("p (a b) -> p a b", b=shape[2])
        return a.rearrange("p (a b c) -> p a b c", b=shape[2], c=shape[3])

    consts = reg(0, 1, F32, [128, 256])
    identB = reg(1, 1, BF16, [128, 512])
    identF = reg(2, 1, F32, [128, 256])
    gamB = reg(3, 4, F32, [128, C])
    betB = reg(7, 4, F32, [128, C])
    boutB = reg(11, 4, F32, [128, C])
    bvB = reg(15, 2, F32, [128, 512])
    cbB = reg(17, 1, F32, [128, 256])
    bqk_col = reg(18, 1, F32, [128, 256])
    rwT = reg(19, 1, BF16, [128, 8, 64])
    stats = reg(20, 1, F32, [128, 256])
    (MU1, RSTD1, NEGMUR1, MUZ, RSTDZ, NEGMURZ, MU2, RSTD2, NEGMUR2,
     RSTD1S, NEGMUR1S) = (0, 16, 32, 48, 64, 80, 96, 112, 128, 144, 160)
    rs_t = reg(21, 1, F32, [128, 256])
    combF = reg(22, 4, BF16, [128, T])
    r_s = reg(26, 1, F32, [128, 256])
    r_sb = reg(27, 1, F32, [128, 256])
    r_scr = reg(28, 2, F32, [128, 512])
    r_gs = reg(30, 1, F32, [128, 256])
    r_msk = reg(31, 1, F32, [128, 256])
    r_c16 = reg(32, 1, F32, [128, 256])
    r_w = reg(33, 1, F32, [128, 256])
    r_ws = reg(34, 1, F32, [128, 256])
    comb_tm = reg(35, 1, BF16, [128, 512])
    ones_bf = reg(36, 1, BF16, [128, 512])
    ones_f = reg(37, 1, F32, [128, 256])
    ones_bf2 = reg(36, 1, BF16, [128, 512])

    lnF = reg(38, 32, BF16, [128, KC, T])
    rw_sb = reg(38, 4, F32, [128, C])
    h_ring = reg(38, 24, BF16, [128, 2, NKC_H, 128])
    x1e_ring = reg(62, 8, F32, [128, 2, C])

    x_ring = reg(70, 8, F32, [128, 2, C])
    cB_sb = reg(70, 4, BF16, [128, T])
    cb2_sb = reg(74, 4, BF16, [128, T])
    lnx_ring = reg(78, 4, BF16, [128, 2, C])

    wqk_sb = reg(82, 16, BF16, [128, KC, 1024])
    wv_sb = reg(98, 8, BF16, [128, KC, 512])
    wout_sb = reg(106, 8, BF16, [128, 4, 1024])
    qF = reg(114, 16, BF16, [128, 4, T])
    lgF = reg(114, 8, F32, [128, T])
    kF = reg(130, 16, BF16, [128, 4, T])
    v_tm = reg(146, 16, BF16, [128, NT, 512])
    oF = reg(162, 16, BF16, [128, 4, T])
    wd_sb = reg(82, 96, BF16, [128, NKC_H, 1024])

    P_A = reg(178, 4, BF16, [128, T])
    P_B = reg(182, 4, BF16, [128, T])
    PtA = reg(186, 4, BF16, [128, T])
    PtB = reg(190, 4, BF16, [128, T])
    z1_ring = reg(178, 8, F32, [128, 2, C])
    scr1 = reg(186, 8, F32, [128, 2, C])
    x1w = reg(194, 8, F32, [128, 2, C])
    sq_dump = reg(202, 2, BF16, [128, 1024])
    swg_ring = reg(178, 8, BF16, [128, 4, KC, 128])
    swu_ring = reg(186, 8, BF16, [128, 4, KC, 128])
    wg_buf = reg(178, 8, BF16, [128, 2, KC, EH])
    wu_buf = reg(186, 8, BF16, [128, 2, KC, EH])
    h_stage = reg(194, 8, BF16, [128, 2, T])
    silu_scr = reg(202, 2, BF16, [128, 2, 512])
    x2_ring = reg(178, 8, F32, [128, 2, C])

    ps = nc.alloc_psum_tensor("ps", [128, 4096], F32)
    B0, B2 = 0, 1024
    TP0, TP1 = 2048, 3072

    CSCL, CEPS, CONE, CZERO, CR2, CINV, CTINY = range(7)

    sched = Sched()
    EMIT = sched.emit

    def dma_sy(dst, src):
        def f(sy):
            return sy.dma_start(out=dst, in_=src)
        return f

    def dma_g(dst, src):
        def f(g):
            return g.dma_start(out=dst, in_=src)
        return f

    def f(v):
        v.memset(consts[:, CSCL:CSCL + 1], 0.125)
        v.memset(consts[:, CEPS:CEPS + 1], LN_EPS)
        v.memset(consts[:, CONE:CONE + 1], 1.0)
        v.memset(consts[:, CZERO:CZERO + 1], 0.0)
        v.memset(consts[:, CR2:CR2 + 1], RSQRT2)
        v.memset(consts[:, CINV:CINV + 1], 1.0 / C)
        v.memset(consts[:, CTINY:CTINY + 1], 1e-20)
        v.memset(ones_bf[0:1, 0:128], 1.0)
        v.memset(ones_bf[:, 256:257], 1.0)
        return v.memset(ones_f[0:1, 0:128], 1.0)
    v_init = EMIT("V", f, inc="V")

    def f(g):
        g.iota(identB[:, 128:256], pattern=[[1, 128]], base=0,
               channel_multiplier=-1, allow_small_or_imprecise_dtypes=True)
        return g.iota(identF[:, 128:256], pattern=[[1, 128]], base=0,
                      channel_multiplier=-1,
                      allow_small_or_imprecise_dtypes=True)
    g_iota = EMIT("G", f, inc="G")

    def f(v):
        v.tensor_single_scalar(identB[:, 0:128], identB[:, 128:256], 0.0,
                               ALU.is_equal)
        return v.tensor_single_scalar(identF[:, 0:128], identF[:, 128:256], 0.0,
                                      ALU.is_equal)
    v_ident = EMIT("V", f, waits=[("G", g_iota)], inc="V")

    EMIT("G", dma_g(wqk_sb[:], wqk.rearrange("(kc p) f -> p kc f", p=128)),
         inc="SG")
    EMIT("G", dma_g(wv_sb[:], wv.rearrange("(kc p) f -> p kc f", p=128)),
         inc="SG")
    sg_attnw = EMIT("G", dma_g(wout_sb[:], wout.rearrange("(kc p) f -> p kc f",
                                                          p=128)), inc="SG")

    EMIT("SY", dma_sy(gamB[0:1, :], gam[:, :]), inc="SA")
    EMIT("SY", dma_sy(betB[0:1, :], bet[:, :]), inc="SA")
    EMIT("SY", dma_sy(boutB[0:1, :], bout[:, :]), inc="SA")
    EMIT("SY", dma_sy(bvB[0:1, 0:512], bv[:, :]), inc="SA")
    EMIT("SY", dma_sy(cbB[0:1, 0:16], cb[:, :]), inc="SA")
    EMIT("SY", dma_sy(rw_sb[0:16, :], rw[:, :]), inc="SA")
    bqk_row = lgF[0:1, 0:1024]
    sa_rows = EMIT("SY", dma_sy(bqk_row, bqk[:, :]), inc="SA")

    prev_v = [v_init]

    def bcast(dst, src_row, n, extra=()):
        t_marks = []
        for j in range(0, n, 512):
            w = min(512, n - j)
            def f(t, j=j, w=w, src_row=src_row):
                return t.matmul(ps[:, j:j + w], ones_f[0:1, 0:128],
                                src_row[:, j:j + w], start=True, stop=True)
            wl = [("SA", sa_rows), ("V", prev_v[0])] + list(extra)
            t_marks.append(EMIT("T", f, waits=wl, inc="T"))
        for i, j in enumerate(range(0, n, 512)):
            w = min(512, n - j)
            def f(v, j=j, w=w, dst=dst):
                return v.tensor_copy(dst[:, j:j + w], ps[:, j:j + w])
            prev_v[0] = EMIT("V", f, waits=[("T", t_marks[i])], inc="V")

    bcast(gamB, gamB[0:1, :], C)
    bcast(betB, betB[0:1, :], C)
    bcast(boutB, boutB[0:1, :], C)
    bcast(bvB, bvB[0:1, 0:512], 512)
    bcast(cbB[:, 0:16], cbB[0:1, 0:16], 16)

    def f(v):
        v.tensor_scalar_mul(gamB[:, :], gamB[:, :], RSQRT2)
        return v.tensor_scalar_mul(betB[:, :], betB[:, :], RSQRT2)
    prev_v[0] = EMIT("V", f, waits=[("V", prev_v[0])], inc="V")

    def f(t):
        li = None
        for m in range(8):
            li = t.matmul(ps[:, 2048 + m:2048 + m + 1],
                          bqk_row[:, 128 * m:128 * (m + 1)],
                          ones_f[0:1, 0:1], start=(m == 0), stop=(m == 7))
        return li
    t_bq = EMIT("T", f, waits=[("SA", sa_rows), ("V", prev_v[0])], inc="T")

    def f(v):
        return v.tensor_copy(bqk_col[:, 0:8], ps[:, 2048:2056])
    EMIT("V", f, waits=[("T", t_bq)], inc="V")

    def f(t):
        li = None
        for c in range(KC):
            li = t.matmul(ps[:, 2560 + 16 * c:2560 + 16 * (c + 1)],
                          rw_sb[0:16, 128 * c:128 * (c + 1)],
                          identF[0:16, 0:16], start=(c == 0), stop=(c == KC - 1))
        return li
    t_rw = EMIT("T", f, waits=[("SA", sa_rows), ("V", v_ident)], inc="T")

    def f(v):
        return v.tensor_copy(rwT[:, :, 0:16],
                             ps[:, 2560:2688].rearrange("p (a b) -> p a b", b=16))
    EMIT("V", f, waits=[("T", t_rw)], inc="V")

    xr, lr, tpr, p0r = Ring(2), Ring(2), Ring(2), Ring(2)
    z1r, hr, x1er, x2r = Ring(2), Ring(2), Ring(2), Ring(2)
    swr, wgr = Ring(4), Ring(2)
    x1wr, scr1r = Ring(2), Ring(2)

    x_r3 = x_in.rearrange("(n p) c -> p n c", p=128)
    y_r3 = y_out.rearrange("(n p) c -> p n c", p=128)
    h_r3 = h_cache.rearrange("(kc p) tok -> p kc tok", p=128)
    x1_r3 = x1_cache.rearrange("(t p) c -> p t c", p=128)

    def ln_stats(MU, RSTD, NEGMUR, t, src, pre_waits, wfn=None):
        def f(v, t=t):
            v.reduce_sum(stats[:, MU + t:MU + t + 1], src, axis=AX)
            return v.scalar_tensor_tensor(scr1[:, 0, :], src, 0.0, src, ALU.bypass,
                                          ALU.mult,
                                          accum_out=stats[:, RSTD + t:RSTD + t + 1])
        wl = (wfn("V", pre_waits) if wfn else list(pre_waits))
        v_a = EMIT("V", f, waits=wl, inc="V")

        def f(v, t=t):
            return v.tensor_scalar_mul(stats[:, MU + t:MU + t + 1],
                                       stats[:, MU + t:MU + t + 1], 1.0 / C)
        v_b = EMIT("V", f, waits=[("V", v_a)], inc="V")

        def f(v, t=t):
            return v.scalar_tensor_tensor(stats[:, NEGMUR + t:NEGMUR + t + 1],
                                          stats[:, MU + t:MU + t + 1], 1.0,
                                          stats[:, MU + t:MU + t + 1],
                                          ALU.mult, ALU.mult)
        v_c = EMIT("V", f, waits=[("V", v_b)], inc="V")

        def f(v, t=t):
            return v.scalar_tensor_tensor(
                stats[:, RSTD + t:RSTD + t + 1],
                stats[:, RSTD + t:RSTD + t + 1], 1.0 / C,
                stats[:, NEGMUR + t:NEGMUR + t + 1], ALU.mult, ALU.subtract)
        v_var = EMIT("V", f, waits=[("V", v_c)], inc="V")

        def f(sc, t=t):
            return sc.activation(stats[:, RSTD + t:RSTD + t + 1],
                                 stats[:, RSTD + t:RSTD + t + 1], AF.Sqrt,
                                 bias=consts[:, CEPS:CEPS + 1])
        s_sq = EMIT("S", f, waits=[("V", v_var)], inc="S")

        def f(v, t=t):
            return v.reciprocal(stats[:, RSTD + t:RSTD + t + 1],
                                stats[:, RSTD + t:RSTD + t + 1])
        v_rc = EMIT("V", f, waits=[("S", s_sq)], inc="V")

        def f(v, t=t):
            return v.scalar_tensor_tensor(
                stats[:, NEGMUR + t:NEGMUR + t + 1],
                stats[:, MU + t:MU + t + 1], -1.0,
                stats[:, RSTD + t:RSTD + t + 1], ALU.mult, ALU.mult)
        return EMIT("V", f, waits=[("V", v_rc)], inc="V")

    for s in range(n_seq):
        BAR = sched.bar()
        barw = [(k, BAR[k]) for k in ("T", "V", "S", "G", "SA", "SG", "SC")]
        first = {e: True for e in ("T", "V", "S", "G", "SY")}

        def W(eng, waits):
            if first[eng]:
                first[eng] = False
                return barw + list(waits)
            return list(waits)

        v_evacA = None
        for t in range(NT):
            xl = EMIT("SY", dma_sy(x_ring[:, t % 2, :], x_r3[:, s * NT + t, :]),
                      waits=W("SY", xr.waits(t)),
                      inc="SA" if t % 2 == 0 else "SA2")

            if first["S"]:
                EMIT("S", lambda sc: sc.nop(), waits=W("S", []))
            v_st = ln_stats(MU1, RSTD1, NEGMUR1, t, x_ring[:, t % 2, :],
                            [("SA" if t % 2 == 0 else "SA2", xl)], wfn=W)

            def f(sc, t=t):
                return sc.activation(lnx_ring[:, t % 2, :], x_ring[:, t % 2, :],
                                     AF.Identity,
                                     bias=stats[:, NEGMUR1 + t:NEGMUR1 + t + 1],
                                     scale=stats[:, RSTD1 + t:RSTD1 + t + 1])
            s_ln1 = EMIT("S", f, waits=lr.waits(t) + [("V", v_st)], inc="S")
            xr.set_free(t, ("S", s_ln1))

            base = TP0 if t % 2 == 0 else TP1
            def f(tn, t=t, base=base):
                li = None
                for c in range(KC):
                    li = tn.matmul(ps[:, base + 128 * c:base + 128 * (c + 1)],
                                   lnx_ring[:, t % 2, 128 * c:128 * (c + 1)],
                                   identB[:, 0:128],
                                   start=(c % 4 == 0), stop=(c % 4 == 3))
                return li
            t_tr = EMIT("T", f,
                        waits=W("T", tpr.waits(t) + [("S", s_ln1),
                                                     ("V", v_ident)]),
                        inc="T")
            lr.set_free(t, ("T", t_tr))

            def f(v, t=t, base=base):
                v.tensor_copy(lnF[:, 0:4, 128 * t:128 * (t + 1)],
                              ps[:, base:base + 512].rearrange(
                                  "p (a b) -> p a b", b=128))
                return v.tensor_copy(lnF[:, 4:8, 128 * t:128 * (t + 1)],
                                     ps[:, base + 512:base + 1024].rearrange(
                                         "p (a b) -> p a b", b=128))
            v_evacA = EMIT("V", f, waits=[("T", t_tr)], inc="V")
            tpr.set_free(t, ("V", v_evacA))
        v_lnF = v_evacA

        for m in range(8):
            for tb in range(4):
                i = m * 4 + tb
                base = B0 if i % 2 == 0 else B2
                wl = p0r.waits(i)
                def f(tn, m=m, tb=tb, base=base):
                    li = None
                    for kc in range(KC):
                        li = tn.matmul(ps[:, base:base + 512],
                                       wqk_sb[:, kc, 128 * m:128 * (m + 1)],
                                       lnF[:, kc, 512 * tb:512 * (tb + 1)],
                                       start=(kc == 0), stop=(kc == KC - 1))
                    return li
                t_mm = EMIT("T", f, waits=wl + [("V", v_lnF), ("SG", sg_attnw)],
                            inc="T")
                dst = qF if m < 4 else kF
                def f(v, m=m, tb=tb, base=base, dst=dst):
                    return v.tensor_scalar(dst[:, m % 4, 512 * tb:512 * (tb + 1)],
                                           ps[:, base:base + 512],
                                           bqk_col[:, m:m + 1], None, ALU.add)
                ev = EMIT("V", f, waits=[("T", t_mm)], inc="V")
                p0r.set_free(i, ("V", ev))
        for t in range(NT):
            base = B0 if t % 2 == 0 else B2
            wl = p0r.waits(t)
            def f(tn, t=t, base=base):
                li = None
                for kc in range(KC):
                    li = tn.matmul(ps[:, base:base + 512],
                                   lnF[:, kc, 128 * t:128 * (t + 1)],
                                   wv_sb[:, kc, :],
                                   start=(kc == 0), stop=(kc == KC - 1))
                return li
            t_mm = EMIT("T", f, waits=wl, inc="T")
            def f(v, t=t, base=base):
                return v.scalar_tensor_tensor(v_tm[:, t, :], ps[:, base:base + 512],
                                              0.0, bvB[:, 0:512], ALU.bypass,
                                              ALU.add)
            ev = EMIT("V", f, waits=[("T", t_mm)], inc="V")
            p0r.set_free(t, ("V", ev))
        v_qkv = sched.now("V")

        def f(v):
            v.tensor_scalar_mul(stats[:, RSTD1S:RSTD1S + 16],
                                stats[:, RSTD1:RSTD1 + 16], RSQRT2)
            return v.tensor_scalar_mul(stats[:, NEGMUR1S:NEGMUR1S + 16],
                                       stats[:, NEGMUR1:NEGMUR1 + 16], RSQRT2)
        v_qkv = EMIT("V", f, waits=[("V", v_qkv)], inc="V")

        bp_V = bp_S = 0
        if USE_BOUND:
            last_red = None
            for which, src_t in ((0, qF), (1, kF)):
                for hp in range(4):
                    def f(sc, hp=hp, src_t=src_t):
                        return sc.activation(P_A[:], src_t[:, hp, :], AF.Square)
                    s_sq2 = EMIT("S", f, waits=[("V", v_qkv)]
                                 + ([("V", last_red)] if last_red else []),
                                 inc="S")
                    for head in range(2):
                        hb = 64 * head
                        hidx = 2 * hp + head
                        colp = 2048 + 2048 * 0
                        def f(tn, hb=hb):
                            li = None
                            for nb in range(4):
                                li = tn.matmul(
                                    ps[0:1, 2048 + 512 * nb:2048 + 512 * (nb + 1)],
                                    ones_bf[hb:hb + 64, 256:257],
                                    P_A[hb:hb + 64, 512 * nb:512 * (nb + 1)],
                                    start=True, stop=True)
                            return li
                        t_nm = EMIT("T", f, waits=[("S", s_sq2)]
                                    + ([("V", last_red)] if last_red else []),
                                    inc="T")
                        def f(v, which=which, hidx=hidx):
                            return v.reduce_max(
                                rs_t[0:1, 32 + 16 * which + hidx:
                                     33 + 16 * which + hidx],
                                ps[0:1, 2048:4096], axis=AX)
                        last_red = EMIT("V", f, waits=[("T", t_nm)], inc="V")
            def f(v):
                return v.tensor_tensor(rs_t[0:1, 64:72], rs_t[0:1, 32:40],
                                       rs_t[0:1, 48:56], ALU.mult)
            v_bm = EMIT("V", f, waits=[("V", last_red)], inc="V")
            def f(sc):
                return sc.activation(rs_t[0:1, 64:72], rs_t[0:1, 64:72], AF.Sqrt)
            s_bq = EMIT("S", f, waits=[("V", v_bm)], inc="S")
            def f(v):
                return v.tensor_scalar_mul(rs_t[0:1, 64:72], rs_t[0:1, 64:72],
                                           -0.125)
            v_bb = EMIT("V", f, waits=[("S", s_bq)], inc="V")
            for hidx in range(8):
                def f(tn, hidx=hidx):
                    return tn.matmul(ps[0:128, 2048 + hidx:2048 + hidx + 1],
                                     ones_f[0:1, 0:128],
                                     rs_t[0:1, 64 + hidx:65 + hidx],
                                     start=(hidx == 0), stop=(hidx == 7))
                EMIT("T", f, waits=[("V", v_bb)],
                     inc="T" if hidx == 7 else None)
            t_bbc = sched.now("T")
            def f(v):
                return v.tensor_copy(rs_t[:, 16:24], ps[:, 2048:2056])
            v_bprep = EMIT("V", f, waits=[("T", t_bbc)], inc="V")
            bp_V, bp_S = v_bprep, sched.now("S")
        v_qkv = sched.now("V")

        for ec in range(3):
            EMIT("G", dma_g(
                wd_sb[:, 4 * ec:4 * (ec + 1), :],
                sdw.rearrange("(kc p) f -> p kc f", p=128)[:, 4 * ec:4 * (ec + 1), :]),
                waits=[("V", v_qkv)] if ec == 0 else [], inc="SG")

        steps = [(hp, qc, head) for hp in range(4) for qc in range(NT)
                 for head in range(2)]
        nsteps = len(steps)
        side_free = {0: [], 1: []}
        p_buf_free = {0: [], 1: []}
        pt_buf_free = {0: [], 1: []}
        st = {}
        pv_evac = None

        def b_head(i):
            hp, qc, head = steps[i]
            p = i % 2
            cb = 2048 * p
            P_t = P_A if p == 0 else P_B
            hb = 64 * head
            def f(tn, hp=hp, qc=qc, cb=cb, hb=hb):
                li = None
                for nb in range(4):
                    li = tn.matmul(ps[:, cb + 512 * nb:cb + 512 * (nb + 1)],
                                   qF[hb:hb + 64, hp, 128 * qc:128 * (qc + 1)],
                                   kF[hb:hb + 64, hp, 512 * nb:512 * (nb + 1)],
                                   start=True, stop=True, tile_position=(hb, 0))
                return li
            wl = list(side_free[p])
            if i == 0:
                wl.append(("V", v_qkv))
            if i <= 1 and USE_BOUND:
                wl.append(("V", bp_V))
            t_sim = EMIT("T", f, waits=wl, inc="T")
            if USE_BOUND:
                hidx = 2 * hp + head
                def f(sc, p=p, cb=cb, P_t=P_t, hidx=hidx):
                    return sc.activation(P_t[:], ps[:, cb:cb + 2048], AF.Exp,
                                         bias=rs_t[:, 16 + hidx:17 + hidx],
                                         scale=consts[:, CSCL:CSCL + 1],
                                         accum_out=rs_t[:, 4 + p:5 + p])
                s_exp = EMIT("S", f, waits=W("S", [("T", t_sim)])
                             + list(p_buf_free[p]), inc="S")
            else:
                def f(v, p=p, cb=cb):
                    return v.reduce_max(rs_t[:, p:p + 1], ps[:, cb:cb + 2048],
                                        axis=AX, negate=True)
                EMIT("V", f, waits=[("T", t_sim)], inc="V")
                def f(v, p=p):
                    return v.tensor_scalar_mul(rs_t[:, 2 + p:3 + p],
                                               rs_t[:, p:p + 1], 0.125)
                v_scale = EMIT("V", f, inc="V")
                def f(sc, p=p, cb=cb, P_t=P_t):
                    return sc.activation(P_t[:], ps[:, cb:cb + 2048], AF.Exp,
                                         bias=rs_t[:, 2 + p:3 + p],
                                         scale=consts[:, CSCL:CSCL + 1],
                                         accum_out=rs_t[:, 4 + p:5 + p])
                s_exp = EMIT("S", f, waits=W("S", [("V", v_scale)])
                             + list(p_buf_free[p]), inc="S")
            st[i] = {"exp": s_exp}

        def b_tail(i):
            hp, qc, head = steps[i]
            p = i % 2
            cb = 2048 * p
            P_t = P_A if p == 0 else P_B
            Pt_t = PtA if p == 0 else PtB
            hb = 64 * head
            def f(v, p=p):
                return v.reciprocal(rs_t[:, 6 + p:7 + p], rs_t[:, 4 + p:5 + p])
            EMIT("V", f, waits=[("S", st[i]["exp"])], inc="V")
            def f(v, p=p, P_t=P_t):
                return v.tensor_scalar_mul(P_t[:], P_t[:], rs_t[:, 6 + p:7 + p])
            v_norm = EMIT("V", f, inc="V")
            t_trq = []
            ptevq = []
            for q4 in range(4):
                def f(tn, cb=cb, P_t=P_t, q4=q4):
                    li = None
                    for j in range(4):
                        jj = 4 * q4 + j
                        col = cb + 128 * jj
                        li = tn.matmul(ps[:, col:col + 128],
                                       P_t[:, 128 * jj:128 * (jj + 1)],
                                       identB[:, 0:128],
                                       start=(j == 0), stop=(j == 3))
                    return li
                wl = [("V", v_norm)] if q4 == 0 else []
                t_trq.append(EMIT("T", f, waits=wl, inc="T"))
                def f(v, cb=cb, Pt_t=Pt_t, q4=q4):
                    return v.tensor_copy(Pt_t[:, 512 * q4:512 * (q4 + 1)],
                                         ps[:, cb + 512 * q4:cb + 512 * (q4 + 1)])
                ptevq.append(EMIT("V", f, waits=[("T", t_trq[q4])]
                                  + (list(pt_buf_free[p]) if q4 == 0 else []),
                                  inc="V"))
            def f(tn, hp=hp, head=head, cb=cb, hb=hb, Pt_t=Pt_t):
                li = None
                for kc in range(8):
                    li = tn.matmul(ps[hb:hb + 64, cb:cb + 128],
                                   v_tm[:, kc, 64 * (2 * hp + head):
                                        64 * (2 * hp + head) + 64],
                                   Pt_t[:, 128 * kc:128 * (kc + 1)],
                                   start=(kc == 0), stop=False,
                                   tile_position=(0, hb))
                return li
            EMIT("T", f, waits=[("V", ptevq[1])], inc="T")
            def f(tn, hp=hp, head=head, cb=cb, hb=hb, Pt_t=Pt_t):
                li = None
                for kc in range(8, NT):
                    li = tn.matmul(ps[hb:hb + 64, cb:cb + 128],
                                   v_tm[:, kc, 64 * (2 * hp + head):
                                        64 * (2 * hp + head) + 64],
                                   Pt_t[:, 128 * kc:128 * (kc + 1)],
                                   start=False, stop=(kc == NT - 1),
                                   tile_position=(0, hb))
                return li
            t_pv = EMIT("T", f, waits=[("V", ptevq[3])], inc="T")
            def f(v, hp=hp, qc=qc, cb=cb, hb=hb):
                return v.tensor_copy(oF[hb:hb + 64, hp, 128 * qc:128 * (qc + 1)],
                                     ps[hb:hb + 64, cb:cb + 128])
            oev = EMIT("V", f, waits=[("T", t_pv)], inc="V")
            side_free[p] = [("V", oev)]
            p_buf_free[p] = [("T", t_trq[3])]
            pt_buf_free[p] = [("T", t_pv)]
            return oev

        b_head(0)
        for i in range(1, nsteps):
            b_head(i)
            pv_evac = b_tail(i - 1)
        pv_evac = b_tail(nsteps - 1)
        v_attn = pv_evac

        cm = {}
        czv = {}
        c2v = {}

        def c_S1(t):
            base = B0 if t % 2 == 0 else B2
            wl = p0r.waits(t)
            def f(tn, t=t, base=base):
                li = None
                for vc in range(4):
                    for nh in range(2):
                        li = tn.matmul(ps[:, base + 512 * nh:base + 512 * (nh + 1)],
                                       oF[:, vc, 128 * t:128 * (t + 1)],
                                       wout_sb[:, vc, 512 * nh:512 * (nh + 1)],
                                       start=(vc == 0), stop=(vc == 3))
                return li
            extra = [("V", v_attn)] if t == 0 else []
            t_z = EMIT("T", f, waits=wl + extra, inc="T")
            wz = z1r.waits(t)
            def f(v, t=t, base=base):
                return v.scalar_tensor_tensor(
                    z1_ring[:, t % 2, :], ps[:, base:base + 1024], 0.0,
                    boutB[:, :], ALU.bypass, ALU.add,
                    accum_out=stats[:, MUZ + t:MUZ + t + 1])
            v_z1 = EMIT("V", f, waits=wz + [("T", t_z)], inc="V")
            p0r.set_free(t, ("V", v_z1))
            def f(sc, t=t):
                return sc.activation(sq_dump[:, :], z1_ring[:, t % 2, :],
                                     AF.Square,
                                     accum_out=stats[:, RSTDZ + t:RSTDZ + t + 1])
            s_sqz = EMIT("S", f, waits=[("V", v_z1)], inc="S")
            x_re = EMIT("SY", dma_sy(x_ring[:, t % 2, :], x_r3[:, s * NT + t, :]),
                        waits=xr.waits(t), inc="SA")
            cm[t] = {"sqz": s_sqz, "xre": x_re}

        def c_chain(pair, MU, RSTD, NEGMUR, s_last):
            a = 2 * pair
            MUs = stats[:, MU + a:MU + a + 2]
            RSs = stats[:, RSTD + a:RSTD + a + 2]
            NMs = stats[:, NEGMUR + a:NEGMUR + a + 2]
            EMIT("V", lambda v: v.tensor_scalar_mul(MUs, MUs, 1.0 / C),
                 waits=[("S", s_last)], inc="V")
            EMIT("V", lambda v: v.scalar_tensor_tensor(NMs, MUs, 1.0, MUs,
                                                       ALU.mult, ALU.mult),
                 inc="V")
            v_c = EMIT("V", lambda v: v.scalar_tensor_tensor(
                RSs, RSs, 1.0 / C, NMs, ALU.mult, ALU.subtract), inc="V")
            s_d = EMIT("S", lambda sc: sc.activation(
                RSs, RSs, AF.Sqrt, bias=consts[:, CEPS:CEPS + 1]),
                waits=[("V", v_c)], inc="S")
            EMIT("V", lambda v: v.reciprocal(RSs, RSs),
                 waits=[("S", s_d)], inc="V")
            return EMIT("V", lambda v: v.scalar_tensor_tensor(
                NMs, MUs, -1.0, RSs, ALU.mult, ALU.mult), inc="V")

        def c_S2(t):
            pair = t // 2
            def f(sc, t=t):
                return sc.activation(scr1[:, t % 2, :], z1_ring[:, t % 2, :],
                                     AF.Identity,
                                     bias=stats[:, NEGMURZ + t:NEGMURZ + t + 1],
                                     scale=stats[:, RSTDZ + t:RSTDZ + t + 1])
            s_zn = EMIT("S", f, waits=scr1r.waits(t) + [("V", czv[pair])],
                        inc="S")
            z1r.set_free(t, ("S", s_zn))
            def f(g, t=t):
                return g.tensor_tensor(scr1[:, t % 2, :], scr1[:, t % 2, :],
                                       gamB[:, :], ALU.mult)
            g_y0 = EMIT("G", f, waits=W("G", [("S", s_zn)]), inc="G")
            def f(g, t=t):
                return g.tensor_tensor(scr1[:, t % 2, :], scr1[:, t % 2, :],
                                       betB[:, :], ALU.add)
            g_y = EMIT("G", f, waits=[("G", g_y0)], inc="G")
            wx = x1wr.waits(t)
            def f(v, t=t):
                return v.tensor_scalar(x1w[:, t % 2, :], x_ring[:, t % 2, :],
                                       stats[:, RSTD1S + t:RSTD1S + t + 1],
                                       stats[:, NEGMUR1S + t:NEGMUR1S + t + 1],
                                       ALU.mult, ALU.add)
            v_l1 = EMIT("V", f, waits=wx + [("SA", cm[t]["xre"])], inc="V")
            def f(v, t=t):
                return v.tensor_tensor(x1w[:, t % 2, :], x1w[:, t % 2, :],
                                       scr1[:, t % 2, :], ALU.add)
            v_t5 = EMIT("V", f, waits=[("G", g_y)], inc="V")
            scr1r.set_free(t, ("V", v_t5))
            def f(v, t=t):
                return v.scalar_tensor_tensor(
                    x1w[:, t % 2, :], x1w[:, t % 2, :], 1.0,
                    x_ring[:, t % 2, :], ALU.bypass, ALU.add,
                    accum_out=stats[:, MU2 + t:MU2 + t + 1])
            v_x1 = EMIT("V", f, inc="V")
            xr.set_free(t, ("V", v_x1))
            st_x1 = EMIT("SY", dma_sy(x1_r3[:, t, :], x1w[:, t % 2, :]),
                         waits=[("V", v_x1)], inc="SA2")
            def f(sc, t=t):
                return sc.activation(sq_dump[:, :], x1w[:, t % 2, :], AF.Square,
                                     accum_out=stats[:, RSTD2 + t:RSTD2 + t + 1])
            s_sq1 = EMIT("S", f, waits=[("V", v_x1)], inc="S")
            cm[t]["sq1"] = s_sq1
            cm[t]["stx1"] = st_x1

        def c_S3(t):
            pair = t // 2
            def f(sc, t=t):
                return sc.activation(lnx_ring[:, t % 2, :], x1w[:, t % 2, :],
                                     AF.Identity,
                                     bias=stats[:, NEGMUR2 + t:NEGMUR2 + t + 1],
                                     scale=stats[:, RSTD2 + t:RSTD2 + t + 1])
            s_ln2 = EMIT("S", f, waits=lr.waits(t) + [("V", c2v[pair])], inc="S")
            x1wr.set_free(t, ("S", s_ln2), ("SA2", cm[t]["stx1"]))
            base2 = TP0 if t % 2 == 0 else TP1
            def f(tn, t=t, base2=base2):
                li = None
                for c in range(KC):
                    li = tn.matmul(ps[:, base2 + 128 * c:base2 + 128 * (c + 1)],
                                   lnx_ring[:, t % 2, 128 * c:128 * (c + 1)],
                                   identB[:, 0:128],
                                   start=(c % 4 == 0), stop=(c % 4 == 3))
                return li
            t_tr2 = EMIT("T", f, waits=tpr.waits(t) + [("S", s_ln2)], inc="T")
            lr.set_free(t, ("T", t_tr2))
            def f(v, t=t, base2=base2):
                v.tensor_copy(lnF[:, 0:4, 128 * t:128 * (t + 1)],
                              ps[:, base2:base2 + 512].rearrange(
                                  "p (a b) -> p a b", b=128))
                return v.tensor_copy(lnF[:, 4:8, 128 * t:128 * (t + 1)],
                                     ps[:, base2 + 512:base2 + 1024].rearrange(
                                         "p (a b) -> p a b", b=128))
            v_ln2F = EMIT("V", f, waits=[("T", t_tr2)], inc="V")
            tpr.set_free(t, ("V", v_ln2F))
            return v_ln2F

        for kk in range(10):
            if kk >= 2:
                v_ln2F = c_S3(2 * kk - 4)
                v_ln2F = c_S3(2 * kk - 3)
            if 1 <= kk <= 8:
                c_S2(2 * kk - 2)
                c_S2(2 * kk - 1)
            if kk <= 7:
                c_S1(2 * kk)
                c_S1(2 * kk + 1)
                czv[kk] = c_chain(kk, MUZ, RSTDZ, NEGMURZ, cm[2 * kk + 1]["sqz"])
            if 1 <= kk <= 8:
                c2v[kk - 1] = c_chain(kk - 1, MU2, RSTD2, NEGMUR2,
                                      cm[2 * kk - 1]["sq1"])
        s_C_end = sched.now("S")
        v_C_end = sched.now("V")
        g_C_end = sched.now("G")
        sa_C_end = sched.now("SA")
        sa2_C_end = sched.now("SA2")

        def f(tn):
            li = None
            for nb in range(4):
                for kc in range(KC):
                    li = tn.matmul(ps[0:16, 2048 + 512 * nb:2048 + 512 * (nb + 1)],
                                   rwT[:, kc, 0:16],
                                   lnF[:, kc, 512 * nb:512 * (nb + 1)],
                                   start=(kc == 0), stop=(kc == KC - 1))
            return li
        t_lg = EMIT("T", f, waits=[("V", v_ln2F)], inc="T")

        def f(v):
            return v.tensor_copy(lgF[0:16, :], ps[0:16, 2048:4096])
        v_lg = EMIT("V", f, waits=[("T", t_lg)], inc="V")

        def f(tn):
            li = None
            for t in range(NT):
                li = tn.matmul(ps[0:128, 2048 + 16 * t:2048 + 16 * (t + 1)],
                               lgF[0:16, 128 * t:128 * (t + 1)],
                               identF[0:16, 0:16],
                               start=(t == 0), stop=(t == NT - 1))
            return li
        t_lgt = EMIT("T", f, waits=[("V", v_lg)], inc="T")

        def f(sc):
            return sc.activation(r_s[:, :], ps[:, 2048:2304], AF.Sigmoid)
        s_sig = EMIT("S", f, waits=[("T", t_lgt)], inc="S")

        def vop(fn, waits=()):
            return EMIT("V", fn, waits=list(waits), inc="V")

        sb3v = r_sb[:, :].rearrange("p (t e) -> p t e", e=16)
        s3v = r_s[:, :].rearrange("p (t e) -> p t e", e=16)
        cb3v = cbB[:, 0:16].rearrange("p (o e) -> p o e", o=1).broadcast_to(
            [128, 16, 16])
        sb4v = r_sb[:, :].rearrange("p (t g j) -> p t g j", g=4, j=4)
        p6v = r_scr[:, 0:384].rearrange("p (t g x) -> p t g x", g=4, x=6)
        gs3v = r_gs[:, 0:64].rearrange("p (t g) -> p t g", g=4)
        cg3v = r_gs[:, 64:128].rearrange("p (t g) -> p t g", g=4)
        tmp3v = r_gs[:, 128:192].rearrange("p (t g) -> p t g", g=4)
        emEv = r_gs[:, 64:128].rearrange("p (t g o) -> p t g o", g=4, o=1) \
            .broadcast_to([128, 16, 4, 4])
        m4v = r_msk[:, :].rearrange("p (t g j) -> p t g j", g=4, j=4)
        m3v = r_msk[:, :].rearrange("p (t e) -> p t e", e=16)
        c3v = r_c16[:, :].rearrange("p (t e) -> p t e", e=16)
        w3v = r_w[:, :].rearrange("p (t e) -> p t e", e=16)
        tAv = r_scr[:, 0:256].rearrange("p (t e) -> p t e", e=16)
        tBv = r_scr[:, 256:512].rearrange("p (t e) -> p t e", e=16)

        vop(lambda v: v.tensor_tensor(sb3v, s3v, cb3v, ALU.add),
            waits=[("S", s_sig)])
        pairs = [(0, 1), (0, 2), (0, 3), (1, 2), (1, 3), (2, 3)]
        def f(v):
            li = None
            for idx, (i, j) in enumerate(pairs):
                li = v.tensor_tensor(p6v[:, :, :, idx], sb4v[:, :, :, i],
                                     sb4v[:, :, :, j], ALU.add)
            return li
        vop(f)
        vop(lambda v: v.tensor_reduce(gs3v, p6v, axis=AX, op=ALU.max))
        vop(lambda v: v.memset(cg3v, 0.0))
        for d in (1, 2, 3):
            def f(v, d=d):
                v.tensor_tensor(tmp3v[:, :, 0:4 - d], gs3v[:, :, d:4],
                                gs3v[:, :, 0:4 - d], ALU.is_gt)
                return v.tensor_tensor(tAv[:, :, 0:4 - d], gs3v[:, :, 0:4 - d],
                                       gs3v[:, :, d:4], ALU.is_gt)
            vop(f)
            vop(lambda v, d=d: v.tensor_tensor(cg3v[:, :, 0:4 - d],
                                               cg3v[:, :, 0:4 - d],
                                               tmp3v[:, :, 0:4 - d], ALU.add))
            vop(lambda v, d=d: v.tensor_tensor(cg3v[:, :, d:4], cg3v[:, :, d:4],
                                               tAv[:, :, 0:4 - d], ALU.add))
        vop(lambda v: v.tensor_single_scalar(cg3v, cg3v, 1.5, ALU.is_lt))
        vop(lambda v: v.tensor_tensor(m4v, sb4v, emEv, ALU.mult))
        vop(lambda v: v.tensor_scalar(tAv, emEv.rearrange("p a g o -> p (a g o)")
                                      if False else
                                      r_gs[:, 64:128].rearrange(
                                          "p (t g o) -> p t g o", g=4, o=1)
                                      .broadcast_to([128, 16, 4, 4]),
                                      1e30, 1e30, ALU.mult, ALU.subtract))
        vop(lambda v: v.tensor_tensor(m3v, m3v,
                                      tAv.rearrange("p a e -> p (a e)")
                                      if False else r_scr[:, 0:256]
                                      .rearrange("p (t e) -> p t e", e=16),
                                      ALU.add))
        vop(lambda v: v.memset(c3v, 0.0))
        for d in range(1, 16):
            def f(v, d=d):
                v.tensor_tensor(tAv[:, :, 0:16 - d], m3v[:, :, d:16],
                                m3v[:, :, 0:16 - d], ALU.is_gt)
                return v.tensor_tensor(tBv[:, :, 0:16 - d], m3v[:, :, 0:16 - d],
                                       m3v[:, :, d:16], ALU.is_gt)
            vop(f)
            vop(lambda v, d=d: v.tensor_tensor(c3v[:, :, 0:16 - d],
                                               c3v[:, :, 0:16 - d],
                                               tAv[:, :, 0:16 - d], ALU.add))
            vop(lambda v, d=d: v.tensor_tensor(c3v[:, :, d:16], c3v[:, :, d:16],
                                               tBv[:, :, 0:16 - d], ALU.add))
        vop(lambda v: v.tensor_single_scalar(r_c16[:, :], r_c16[:, :], 3.5,
                                             ALU.is_lt))
        vop(lambda v: v.tensor_tensor(r_w[:, :], r_s[:, :], r_c16[:, :],
                                      ALU.mult))
        vop(lambda v: v.tensor_reduce(r_ws[:, 0:16], w3v, axis=AX, op=ALU.add))
        vop(lambda v: v.tensor_scalar_add(r_ws[:, 0:16], r_ws[:, 0:16], 1e-20))
        vop(lambda v: v.reciprocal(r_ws[:, 16:32], r_ws[:, 0:16]))
        wr3v = r_ws[:, 16:32].rearrange("p (e o) -> p e o", o=1).broadcast_to(
            [128, 16, 16])
        ct3v = comb_tm[:, 0:256].rearrange("p (t e) -> p t e", e=16)
        v_route = vop(lambda v: v.tensor_tensor(ct3v, w3v, wr3v, ALU.mult))

        def f(tn):
            li = None
            for t in range(NT):
                li = tn.matmul(ps[0:16, 2048 + 128 * t:2048 + 128 * (t + 1)],
                               comb_tm[:, 16 * t:16 * (t + 1)],
                               identB[:, 0:128],
                               start=(t % 4 == 0), stop=(t % 4 == 3))
            return li
        t_cb = EMIT("T", f, waits=[("V", v_route)], inc="T")

        def f(v):
            return v.tensor_copy(combF[0:16, :], ps[0:16, 2048:4096])
        v_combF = EMIT("V", f, waits=[("T", t_cb)], inc="V")

        sg_r = sgw.rearrange("(kc p) hh -> p kc hh", p=128)
        su_r = suw.rearrange("(kc p) hh -> p kc hh", p=128)
        hst_free = {0: (), 1: ()}
        sil_free = {0: (), 1: ()}
        pp_free = {0: [], 1: []}
        uidx = 0
        t_gu = None
        for hc in range(SHID // 128):
            wl = swr.waits(hc)
            if hc < 2:
                wl += [("S", s_C_end), ("V", v_C_end)]
            ldg = EMIT("G", dma_g(swg_ring[:, hc % 4, :, :],
                                  sg_r[:, :, 128 * hc:128 * (hc + 1)]),
                       waits=wl, inc="SG")
            ldu = EMIT("G", dma_g(swu_ring[:, hc % 4, :, :],
                                  su_r[:, :, 128 * hc:128 * (hc + 1)]),
                       inc="SG2")
            ev_last = None
            for tb in range(4):
                par = uidx % 2
                gc = 512 * par
                uc = 1024 + 512 * par
                def f(tn, hc=hc, tb=tb, gc=gc, uc=uc):
                    li = None
                    for kc in range(KC):
                        tn.matmul(ps[:, gc:gc + 512], swg_ring[:, hc % 4, kc, :],
                                  lnF[:, kc, 512 * tb:512 * (tb + 1)],
                                  start=(kc == 0), stop=(kc == KC - 1))
                    for kc in range(KC):
                        li = tn.matmul(ps[:, uc:uc + 512],
                                       swu_ring[:, hc % 4, kc, :],
                                       lnF[:, kc, 512 * tb:512 * (tb + 1)],
                                       start=(kc == 0), stop=(kc == KC - 1))
                    return li
                wl = [("SG", ldg), ("SG2", ldu)] + list(pp_free[par])
                if tb == 0:
                    wl += list(hst_free[hc % 2])
                    if hc == 0:
                        wl += [("SA", sa_C_end), ("SA2", sa2_C_end)]
                t_gu = EMIT("T", f, waits=wl, inc="T")

                def f(sc, par=par):
                    return sc.activation(silu_scr[:, par, :],
                                         ps[:, 512 * par:512 * par + 512],
                                         AF.Silu)
                s_sg = EMIT("S", f, waits=[("T", t_gu)] + list(sil_free[par]),
                            inc="S")

                def f(v, hc=hc, tb=tb, par=par, uc=uc):
                    return v.tensor_tensor(
                        h_stage[:, hc % 2, 512 * tb:512 * (tb + 1)],
                        silu_scr[:, par, :], ps[:, uc:uc + 512], ALU.mult)
                ev_last = EMIT("V", f, waits=[("S", s_sg)], inc="V")
                pp_free[par] = [("V", ev_last)]
                sil_free[par] = (("V", ev_last),)
                uidx += 1
            swr.set_free(hc, ("T", t_gu))
            st_h = EMIT("SY", dma_sy(h_r3[:, hc, :], h_stage[:, hc % 2, :]),
                        waits=[("V", ev_last)], inc="SA")
            hst_free[hc % 2] = (("SA", st_h),)
        t_shared_end = t_gu

        v_cb = None
        cb_free = {0: (("V", v_combF),), 1: (("V", v_combF),)}
        for e in range(E):
            cb_t = cB_sb if e % 2 == 0 else cb2_sb
            wl = list(cb_free[e % 2])
            if e < 2:
                wl.append(("V", v_C_end))
            sa_cst = EMIT("G", dma_g(cb_t[0:1, :], combF[e:e + 1, :]),
                          waits=wl, inc="SD")
            wl = wgr.waits(e)
            if e < 2:
                wl += [("T", t_shared_end), ("V", v_C_end), ("G", g_C_end)]
            ldg = EMIT("G", dma_g(wg_buf[:, e % 2, :, :],
                                  gw[e].rearrange("(kc p) hh -> p kc hh", p=128)),
                       waits=wl, inc="SG")
            ldu = EMIT("G", dma_g(wu_buf[:, e % 2, :, :],
                                  uw[e].rearrange("(kc p) hh -> p kc hh", p=128)),
                       inc="SG2")
            sg_wd = EMIT("G", dma_g(wd_sb[:, 16 + 2 * e:18 + 2 * e, :],
                                    dw[e].rearrange("(kc p) f -> p kc f", p=128)),
                         waits=[("T", t_lgt)] if e == 0 else [], inc="SG")
            if e == 3:
                sg_wd = EMIT("G", dma_g(
                    wd_sb[:, 4 * e:4 * (e + 1), :],
                    sdw.rearrange("(kc p) f -> p kc f", p=128)[:, 4 * e:4 * (e + 1), :]),
                    inc="SG")

            def f(tn, e=e, cb_t=cb_t):
                li = None
                for nb in range(4):
                    li = tn.matmul(ps[:, 2048 + 512 * nb:2048 + 512 * (nb + 1)],
                                   ones_bf[0:1, 0:128],
                                   cb_t[0:1, 512 * nb:512 * (nb + 1)],
                                   start=True, stop=True)
                return li
            t_cbb = EMIT("T", f,
                         waits=[("SD", sa_cst)]
                         + ([("V", v_cb)] if v_cb else []), inc="T")

            def f(v, cb_t=cb_t):
                return v.tensor_copy(cb_t[:, :], ps[:, 2048:4096])
            v_cb = EMIT("V", f, waits=[("T", t_cbb)], inc="V")

            for hc in range(2):
                ev_last = None
                for tb in range(4):
                    par = uidx % 2
                    gc = 512 * par
                    uc = 1024 + 512 * par
                    usc = 1024 + 512 * par
                    def f(tn, e=e, hc=hc, tb=tb, gc=gc, uc=uc):
                        li = None
                        for kc in range(KC):
                            tn.matmul(ps[:, gc:gc + 512],
                                      wg_buf[:, e % 2, kc, 128 * hc:128 * (hc + 1)],
                                      lnF[:, kc, 512 * tb:512 * (tb + 1)],
                                      start=(kc == 0), stop=(kc == KC - 1))
                        for kc in range(KC):
                            li = tn.matmul(ps[:, uc:uc + 512],
                                           wu_buf[:, e % 2, kc,
                                                  128 * hc:128 * (hc + 1)],
                                           lnF[:, kc, 512 * tb:512 * (tb + 1)],
                                           start=(kc == 0), stop=(kc == KC - 1))
                        return li
                    wl = [("SG", ldg), ("SG2", ldu)] + list(pp_free[par])
                    if tb == 0:
                        wl += list(hst_free[hc % 2])
                        if hc == 0:
                            wl.append(("V", v_cb))
                    t_gu = EMIT("T", f, waits=wl, inc="T")

                    def f(sc, par=par):
                        return sc.activation(silu_scr[:, par, :],
                                             ps[:, 512 * par:512 * par + 512],
                                             AF.Silu)
                    s_sg = EMIT("S", f,
                                waits=[("T", t_gu)] + list(sil_free[par]),
                                inc="S")

                    def f(v, hc=hc, tb=tb, uc=uc, cb_t=cb_t):
                        return v.tensor_tensor(
                            h_stage[:, hc % 2, 512 * tb:512 * (tb + 1)],
                            ps[:, uc:uc + 512],
                            cb_t[:, 512 * tb:512 * (tb + 1)], ALU.mult)
                    ev_m = EMIT("V", f, waits=[("S", s_sg)], inc="V")

                    def f(v, hc=hc, tb=tb, par=par):
                        return v.tensor_tensor(
                            h_stage[:, hc % 2, 512 * tb:512 * (tb + 1)],
                            h_stage[:, hc % 2, 512 * tb:512 * (tb + 1)],
                            silu_scr[:, par, :], ALU.mult)
                    ev_last = EMIT("V", f, waits=[("V", ev_m)], inc="V")
                    pp_free[par] = [("S", s_sg), ("V", ev_m)]
                    sil_free[par] = (("V", ev_last),)
                    uidx += 1
                st_h = EMIT("SY", dma_sy(h_r3[:, 16 + 2 * e + hc, :],
                                         h_stage[:, hc % 2, :]),
                            waits=[("V", ev_last)], inc="SA")
                hst_free[hc % 2] = (("SA", st_h),)
            cb_free[e % 2] = (("V", ev_m),)
            wgr.set_free(e, ("T", t_gu))
        t_gu_end = t_gu
        sa_h_done = sched.now("SA")
        s_D_end = sched.now("S")
        v_D_end = sched.now("V")


        for t in range(NT):
            wl = hr.waits(t)
            if t < 2:
                wl += [("T", t_gu_end), ("SA", sa_h_done)]
            ld_h = EMIT("S", dma_sy(h_ring[:, t % 2, :, :],
                                    h_r3[:, :, 128 * t:128 * (t + 1)]),
                        waits=wl, inc="SC" if t % 2 == 0 else "SC2")
            wl = x1er.waits(t)
            if t < 2:
                wl += [("T", t_gu_end)]
            ld_x1 = EMIT("SY", dma_sy(x1e_ring[:, t % 2, :], x1_r3[:, t, :]),
                         waits=wl, inc="SA")
            base = B0 if t % 2 == 0 else B2
            wl = p0r.waits(t)
            if t < 2:
                wl += [("S", s_D_end), ("V", v_D_end)]
            def f(tn, t=t, base=base):
                li = None
                for kc in range(NKC_H):
                    for nh in range(2):
                        li = tn.matmul(ps[:, base + 512 * nh:base + 512 * (nh + 1)],
                                       h_ring[:, t % 2, kc, :],
                                       wd_sb[:, kc, 512 * nh:512 * (nh + 1)],
                                       start=(kc == 0), stop=(kc == NKC_H - 1))
                return li
            t_d = EMIT("T", f, waits=wl + [(("SC" if t % 2 == 0 else "SC2"), ld_h), ("SG", sg_wd)], inc="T")
            hr.set_free(t, ("T", t_d))

            def f(v, t=t, base=base):
                return v.scalar_tensor_tensor(x2_ring[:, t % 2, :],
                                              ps[:, base:base + 1024], 0.0,
                                              x1e_ring[:, t % 2, :],
                                              ALU.bypass, ALU.add)
            v_x2 = EMIT("V", f, waits=x2r.waits(t) + [("T", t_d), ("SA", ld_x1)],
                        inc="V")
            p0r.set_free(t, ("V", v_x2))
            x1er.set_free(t, ("V", v_x2))

            st = EMIT("SY", dma_sy(y_r3[:, s * NT + t, :], x2_ring[:, t % 2, :]),
                      waits=[("V", v_x2)], inc="SA2")
            x2r.set_free(t, ("SA2", st))

        if s < n_seq - 1:
            t_end_E = sched.now("T")
            EMIT("G", dma_g(wqk_sb[:], wqk.rearrange("(kc p) f -> p kc f", p=128)),
                 waits=[("T", t_end_E)], inc="SG")
            EMIT("G", dma_g(wv_sb[:], wv.rearrange("(kc p) f -> p kc f", p=128)),
                 inc="SG")
            sg_attnw = EMIT("G", dma_g(wout_sb[:],
                                       wout.rearrange("(kc p) f -> p kc f", p=128)),
                            inc="SG")

        endbar = sched.bar()
        deps = tuple((k, endbar[k])
                     for k in ("T", "V", "S", "SA", "SC", "SA2", "SC2"))
        for r in (lr, tpr, p0r, z1r, hr, x1er, x2r, swr, wgr, x1wr, scr1r):
            r.reset(*deps)
        xr.reset(cb_free[0][0], cb_free[1][0])

    sa_final = sched.now("SA")
    sa2_final = sched.now("SA2")

    def f(g):
        return g.nop()
    EMIT("G", f, waits=[("SA", sa_final), ("SA2", sa2_final)])

    with (
        nc.semaphore("semT") as semT,
        nc.semaphore("semV") as semV,
        nc.semaphore("semS") as semS,
        nc.semaphore("semG") as semG,
        nc.semaphore("semSA") as semSA,
        nc.semaphore("semSG") as semSG,
        nc.semaphore("semSC") as semSC,
        nc.semaphore("semSA2") as semSA2,
        nc.semaphore("semSG2") as semSG2,
        nc.semaphore("semSC2") as semSC2,
        nc.semaphore("semSD") as semSD,
        nc.Block() as blk,
    ):
        sems = {"T": semT, "V": semV, "S": semS, "G": semG, "SA": semSA,
                "SG": semSG, "SC": semSC, "SA2": semSA2, "SG2": semSG2,
                "SC2": semSC2, "SD": semSD}

        @blk.tensor
        def _(eng):
            _emit_engine(sched, "T", eng, sems)

        @blk.vector
        def _(eng):
            _emit_engine(sched, "V", eng, sems)

        @blk.scalar
        def _(eng):
            _emit_engine(sched, "S", eng, sems)

        @blk.gpsimd
        def _(eng):
            _emit_engine(sched, "G", eng, sems)

        @blk.sync
        def _(eng):
            _emit_engine(sched, "SY", eng, sems)

    return nc


def _host_prep(inputs):
    W_qk = np.asarray(inputs["L0_attn_W_qk"])
    perm = np.empty(1024, dtype=np.int64)
    for h in range(H):
        for d in range(DH):
            perm[h * DH + d] = h * 128 + 2 * d
            perm[512 + h * DH + d] = h * 128 + 2 * d + 1
    return {
        "wqk": np.ascontiguousarray(W_qk[:, perm]),
        "bqk": np.ascontiguousarray(np.asarray(inputs["L0_attn_b_qk"])[perm])[None, :],
        "wv": np.ascontiguousarray(inputs["L0_attn_W_v"]),
        "bv": np.asarray(inputs["L0_attn_b_v"])[None, :],
        "wout": np.ascontiguousarray(inputs["L0_attn_W_out"]),
        "bout": np.asarray(inputs["L0_attn_b_out"])[None, :],
        "gam": np.asarray(inputs["L0_attn_ln_gamma"])[None, :],
        "bet": np.asarray(inputs["L0_attn_ln_beta"])[None, :],
        "rw": np.ascontiguousarray(inputs["L0_ffn_router_w"]),
        "cb": np.asarray(inputs["L0_ffn_correction_bias"])[None, :],
        "gw": np.ascontiguousarray(inputs["L0_ffn_gate_w"]),
        "uw": np.ascontiguousarray(inputs["L0_ffn_up_w"]),
        "dw": np.ascontiguousarray(inputs["L0_ffn_down_w"]),
        "sgw": np.ascontiguousarray(np.asarray(inputs["L0_ffn_shared_gate_w"]).T),
        "suw": np.ascontiguousarray(np.asarray(inputs["L0_ffn_shared_up_w"]).T),
        "sdw": np.ascontiguousarray(np.asarray(inputs["L0_ffn_shared_down_w"]).T),
    }


def kernel(**inputs):
    x = np.asarray(inputs["L0_attn_x"])
    com = _host_prep(inputs)
    nc = build_nc(NSEQ)
    in_maps = []
    for c in range(NCORES):
        m = dict(com)
        m["x"] = np.ascontiguousarray(
            x[c * NSEQ:(c + 1) * NSEQ].reshape(NSEQ * T, C))
        in_maps.append(m)
    res = run_bass_kernel_spmd(nc, in_maps, core_ids=list(range(NCORES)))
    out = np.empty((NCORES * NSEQ, T, C), dtype=np.float32)
    for c in range(NCORES):
        out[c * NSEQ:(c + 1) * NSEQ] = np.asarray(
            res.results[c]["y"], dtype=np.float32).reshape(NSEQ, T, C)
    return out
